# version 17
# baseline (speedup 1.0000x reference)
import numpy as np
import ml_dtypes

import concourse.bass as bass
import concourse.mybir as mybir
from concourse.bass_utils import run_bass_kernel_spmd

BF16 = ml_dtypes.bfloat16
F32 = mybir.dt.float32
BF = mybir.dt.bfloat16
ALU = mybir.AluOpType
AF = mybir.ActivationFunctionType
AX = mybir.AxisListType

NF4_T = [-0.84375, -0.60546875, -0.45703125, -0.337890625, -0.234375,
         -0.1376953125, -0.04541015625, 0.0400390625, 0.12060546875,
         0.2041015625, 0.29296875, 0.390625, 0.50390625, 0.64453125,
         0.86328125]
NF4_D = [0.3046875, 0.171875, 0.12890625, 0.109375, 0.1005859375,
         0.09326171875, 0.09130859375, 0.07958984375, 0.08154296875,
         0.0849609375, 0.091796875, 0.103515625, 0.12109375, 0.16015625,
         0.27734375]
RND_C = 12582912.0
C0 = 20.0
SCALE = float(128) ** -0.5
D = 2048
DFF = 5632
TMY = 1024
TS = 2048
H = 16
NBLK = DFF * D // 64
EPS = 1e-5
SLICE_EL = 2048 * 704

_CACHED = {}


class Ctr:
    __slots__ = ("h", "n")

    def __init__(self, h):
        self.h = h
        self.n = 0

    def inc(self, k=1):
        self.n += k
        return self.n


class Plan:
    ENGINES = ("tensor", "vector", "scalar", "sync", "gpsimd")

    def __init__(self):
        self.ops = {e: [] for e in self.ENGINES}

    def t(self, f):
        self.ops["tensor"].append(f)

    def v(self, f):
        self.ops["vector"].append(f)

    def s(self, f):
        self.ops["scalar"].append(f)

    def y(self, f):
        self.ops["sync"].append(f)

    def g(self, f):
        self.ops["gpsimd"].append(f)


def build_nc(ph=99, p1sub=99, dbg=False):
    nc = bass.Bass("TRN2", target_bir_lowering=False,
                   detect_race_conditions=False)

    xt = nc.dram_tensor("xt", [D, TS], F32, kind="ExternalInput").ap()
    wts = nc.dram_tensor("wts", [4, 256, D], F32, kind="ExternalInput").ap()
    bcols = nc.dram_tensor("bcols", [128, 48], F32, kind="ExternalInput").ap()
    bvrep = nc.dram_tensor("bvrep", [128, D], F32, kind="ExternalInput").ap()
    ballt = nc.dram_tensor("ballt", [D, 80], F32, kind="ExternalInput").ap()
    bot = nc.dram_tensor("bot", [D, 16], F32, kind="ExternalInput").ap()
    at4 = nc.dram_tensor("at4", [80, 4096], F32, kind="ExternalInput").ap()
    w1s = nc.dram_tensor("w1s", [704, D], BF, kind="ExternalInput").ap()
    w2s = nc.dram_tensor("w2s", [704, D], BF, kind="ExternalInput").ap()
    w3s = nc.dram_tensor("w3s", [256, DFF], BF, kind="ExternalInput").ap()
    eyef_d = nc.dram_tensor("eyef_d", [128, 128], F32, kind="ExternalInput").ap()
    onesf_d = nc.dram_tensor("onesf_d", [128, 1], F32, kind="ExternalInput").ap()
    onesrow_d = nc.dram_tensor("onesrow_d", [1, 128], F32,
                               kind="ExternalInput").ap()
    out = nc.dram_tensor("out", [D, TMY], F32, kind="ExternalOutput").ap()
    dbg_specs = [("q_hbm", [D, TMY], BF), ("k_hbm", [D, TS], BF),
                 ("vt_hbm", [TS, D], BF), ("att_h", [D, TMY], F32),
                 ("sums_h", [H, TMY], F32), ("r_h", [D, TMY], F32),
                 ("x2f_h", [D, TMY], F32), ("r2_h", [D, TMY], F32),
                 ("ds_h", [3, 22528], BF), ("agd0_in", [SLICE_EL], BF),
                 ("agd2_in", [SLICE_EL], BF), ("acts_d", [80, 2048], BF),
                 ("s1_h", [3, 22528], BF), ("psc_d", [128, 16], F32),
                 ("d88f", [3, 88, 768], F32), ("d88b", [3, 88, 1216], BF)]
    dbg_out = {}
    if dbg:
        for nm, shp, dt_ in dbg_specs:
            dbg_out[nm] = nc.dram_tensor("d_" + nm, shp, dt_,
                                         kind="ExternalOutput").ap()

    xbf_h = nc.dram_tensor("xbf_h", [D, TS], BF).ap()
    q_hbm = nc.dram_tensor("q_hbm", [D, TMY], BF).ap()
    k_hbm = nc.dram_tensor("k_hbm", [D, TS], BF).ap()
    vt_hbm = nc.dram_tensor("vt_hbm", [TS, D], BF).ap()
    att_h = nc.dram_tensor("att_h", [D, TMY], F32).ap()
    sums_h = nc.dram_tensor("sums_h", [H, TMY], F32).ap()
    r_h = nc.dram_tensor("r_h", [D, TMY], F32).ap()
    x2f_h = nc.dram_tensor("x2f_h", [D, TMY], F32).ap()
    r2_h = nc.dram_tensor("r2_h", [D, TMY], F32).ap()
    rcp_h = nc.dram_tensor("rcp_h", [H, TMY], F32).ap()
    s1_h = nc.dram_tensor("s1_h", [3, 22528], BF).ap()
    ds_h = nc.dram_tensor("ds_h", [3, 22528], BF).ap()
    agw_in = [nc.dram_tensor(f"agw{w}_in", [256, D], BF).ap()
              for w in range(4)]
    agw_out = [nc.dram_tensor(f"agw{w}_out", [8 * 256, D], BF,
                              addr_space="Shared").ap() for w in range(4)]
    agp_in = nc.dram_tensor("agp_in", [1, 8], F32).ap()
    agp_out = nc.dram_tensor("agp_out", [8, 8], F32, addr_space="Shared").ap()
    agd_in = [nc.dram_tensor(f"agd{w}_in", [SLICE_EL], BF).ap()
              for w in range(3)]
    agd_out = [nc.dram_tensor(f"agd{w}_out", [8, SLICE_EL], BF,
                              addr_space="Shared").ap() for w in range(3)]

    from contextlib import ExitStack
    ctx = ExitStack()

    def sb(name, shape, dtype):
        return ctx.enter_context(nc.sbuf_tensor(name, shape, dtype))

    U = sb("U", [128, 45056], BF)
    x2b = sb("x2b", [128, 16384], BF)
    wcolA = sb("wcolA", [128, 4096], BF)
    wcolB = sb("wcolB", [128, 4096], BF)
    qh2 = sb("qh2", [128, 1024], BF)
    pt3 = sb("pt3", [128, 1536], BF)
    sA = sb("sA", [128, 1024], F32)
    sB = sb("sB", [128, 1024], F32)
    sC = sb("sC", [128, 1024], F32)
    rx2 = sb("rx2", [128, 2048], F32)
    acts = sb("acts", [80, 2048], BF)
    acts_o = sb("acts_o", [16, 1024], BF)
    ballt_sb = sb("ballt_sb", [128, 1280], BF)
    bot_sb = sb("bot_sb", [128, 256], BF)
    at4_sb = sb("at4_sb", [80, 4096], BF)
    bcols_sb = sb("bcols_sb", [128, 48], F32)
    onesrb = sb("onesrb", [1, 128], BF)
    eyef = sb("eyef", [128, 128], F32)
    onesf = sb("onesf", [128, 1], F32)
    onesb = sb("onesb", [128, 1], BF)
    onesrow = sb("onesrow", [1, 128], F32)
    lnrow = sb("lnrow", [1, 2048], F32)
    dsc88f = sb("dsc88f", [88, 768], F32)
    psc = sb("psc", [128, 16], F32)
    dum = sb("dum", [1, 16], F32)
    ds1f = sb("ds1f_t", [128, 32], F32)
    dsrec = sb("dsrec_t", [128, 32], F32)
    dstf = sb("dstf_t", [128, 32], F32)
    dwp = sb("dwp_t", [128, 2048], F32)
    dwt = x2b[:, 0:4096]
    dacc = x2b[:, 4096:6144]
    dplane = x2b[:, 6144:8192]
    dstg = x2b[:, 8192:10240]
    ds1 = x2b[:, 14336:14912]
    dst2 = x2b[:, 14912:14976]
    dsc88 = x2b[0:88, 15168:16384]
    kh2 = U[:, 0:4096]
    vt2 = U[:, 4096:8192]
    sums16 = rx2[0:16, 0:1024]
    recip16 = rx2[0:16, 1024:2048]
    bvrep_sb = rx2

    P2B = [ctx.enter_context(nc.psum_tensor(f"PSP{j}", [128, 1024], F32))
           for j in range(4)]
    PS = [P2B[i // 2][:, 512 * (i % 2):512 * (i % 2) + 512] for i in range(8)]

    sems = {}
    for name in ("q_in", "q_g", "q_sc", "q_v", "q_out", "cc", "pe", "ve",
                 "se", "bt", "bv", "bs", "by", "bg"):
        sems[name] = Ctr(ctx.enter_context(nc.semaphore(name)))
    q_in, q_g, q_sc = sems["q_in"], sems["q_g"], sems["q_sc"]
    q_out, cc = sems["q_out"], sems["cc"]
    q_v = sems["q_v"]
    pe, ve, se = sems["pe"], sems["ve"], sems["se"]

    P = Plan()

    def emit_block():
        with nc.Block() as block:
            @block.tensor
            def _(e):
                for fn in P.ops["tensor"]:
                    fn(e)

            @block.vector
            def _(e):
                for fn in P.ops["vector"]:
                    fn(e)

            @block.scalar
            def _(e):
                for fn in P.ops["scalar"]:
                    fn(e)

            @block.sync
            def _(e):
                for fn in P.ops["sync"]:
                    fn(e)

            @block.gpsimd
            def _(e):
                for fn in P.ops["gpsimd"]:
                    fn(e)
        ctx.close()

    bsem = {"tensor": sems["bt"], "vector": sems["bv"], "scalar": sems["bs"],
            "sync": sems["by"], "gpsimd": sems["bg"]}
    bar_n = [0]

    def barrier():
        bar_n[0] += 1
        n = bar_n[0]
        for e in Plan.ENGINES:
            P.ops[e].append(lambda eng, s=bsem[e]: eng.nop().then_inc(s.h, 1))
        for e in Plan.ENGINES:
            for o in Plan.ENGINES:
                if o != e:
                    P.ops[e].append(
                        lambda eng, s=bsem[o], v=n: eng.wait_ge(s.h, v))

    def W(put, sem, val):
        if val and val > 0:
            put(lambda e, s=sem, v=val: e.wait_ge(s.h, v))

    def dma(put, q, dst, src):
        n = q.inc(16)
        put(lambda e, dd=dst, ss=src, qq=q: e.dma_start(out=dd, in_=ss)
            .then_inc(qq.h, 16))
        return n

    def t_mm(outap, lhsT, rhs, start, stop, inc=False):
        if inc:
            n = pe.inc(1)
            P.t(lambda e, o=outap, l=lhsT, r=rhs, st=start, sp=stop:
                e.matmul(o, l, r, start=st, stop=sp).then_inc(pe.h, 1))
            return n
        P.t(lambda e, o=outap, l=lhsT, r=rhs, st=start, sp=stop:
            e.matmul(o, l, r, start=st, stop=sp))
        return None

    def v_op(f, inc=False):
        if inc:
            n = ve.inc(1)
            P.v(lambda e, ff=f: ff(e).then_inc(ve.h, 1))
            return n
        P.v(lambda e, ff=f: ff(e))
        return None

    def v_seq(f):
        W(P.v, ve, ve.n)
        return v_op(f, inc=True)

    def fence(put, q):
        return dma(put, q, dum[0:1, 0:1], onesf_d[0:1, 0:1])

    def s_op(f, inc=False):
        if inc:
            n = se.inc(1)
            P.s(lambda e, ff=f: ff(e).then_inc(se.h, 1))
            return n
        P.s(lambda e, ff=f: ff(e))
        return None

    def xv(i):
        return U[:, 2048 * i:2048 * (i + 1)]

    def finish_early():
        nv = v_op(lambda e: e.memset(sA[:], 0.0), inc=True)
        W(P.y, ve, nv)
        for k in range(16):
            for f in range(2):
                dma(P.y, q_out, out[128 * k:128 * (k + 1),
                                    512 * f:512 * (f + 1)],
                    sA[:, 512 * f:512 * f + 512])
        dma(P.y, q_out, r_h[0:1, 0:1], dum[0:1, 0:1])
        P.y(lambda e: e.wait_ge(q_out.h, 16 * 33))
        emit_block()

    for dst, src in ((eyef[:], eyef_d[:]), (onesf[:], onesf_d[:]),
                     (onesrow[:], onesrow_d[:]), (bcols_sb[:], bcols[:])):
        dma(P.y, q_in, dst, src)
    n_const = q_in.n

    dma(P.g, q_g, onesb[:], onesf_d[:])
    dma(P.g, q_g, onesrb[:], onesrow_d[:])
    n_bvrep = dma(P.y, q_in, bvrep_sb[:, :], bvrep[:])
    g_x = dma(P.g, q_g, xbf_h[:], xt[:])
    dma(P.g, q_g,
        ballt_sb[:].rearrange("p (i c) -> p i c", c=80),
        ballt[:].rearrange("(i p) c -> p i c", p=128))
    dma(P.g, q_g,
        bot_sb[:].rearrange("p (i c) -> p i c", c=16),
        bot[:].rearrange("(i p) c -> p i c", p=128))
    g_small = dma(P.g, q_g, at4_sb[:], at4[:])
    for w in range(4):
        g_w = dma(P.g, q_g, agw_in[w], wts[w])

    def collective(ins_, outs_):
        n = cc.inc(1)
        if n > 1:
            P.g(lambda e, v=n - 1: e.wait_ge(cc.h, v))
        P.g(lambda e, i=ins_, o=outs_: e.collective_compute(
            "AllGather", ALU.bypass, replica_groups=[list(range(8))],
            ins=[i], outs=[o]).then_inc(cc.h, 1))
        return n

    g_w = fence(P.g, q_g)
    W(P.g, q_g, g_w)
    agw_n = []
    for w in range(4):
        agw_n.append(collective(agw_in[w].opt(), agw_out[w].opt()))

    DT = []
    for r in range(6):
        p = 128 if r < 5 else 64
        DT.append((0, w1s, 128 * r, p, 0, 2048, 32, 32 * r))
    for r in range(6):
        p = 128 if r < 5 else 64
        DT.append((1, w2s, 128 * r, p, 0, 2048, 32, 32 * r + 192))
    for r in range(2):
        for (c0, ncols) in ((0, 2048), (2048, 2048), (4096, 1536)):
            DT.append((2, w3s, 128 * r, 128, c0, ncols, 88,
                       384 + 88 * r + c0 // 64))

    v_op(lambda e: e.memset(psc[:], 0.0))
    v_op(lambda e: e.memset(psc[:, 15:16], -C0))
    vred = []
    ld_n = []

    def _emit_passA_load(di):
        w, wd, r0, p, c0, ncols, segw, s1c0 = DT[di]
        buf = di % 2
        if di >= 2:
            W(P.y, ve, vred[di - 2])
        ld_n.append(dma(P.y, q_in, dwt[0:p, 2048 * buf:2048 * buf + ncols],
                        wd[r0:r0 + p, c0:c0 + ncols]))

    _emit_passA_load(0)
    _emit_passA_load(1)
    for di, (w, wd, r0, p, c0, ncols, segw, s1c0) in enumerate(DT):
        buf = di % 2
        nseg = ncols // 64
        W(P.v, q_in, ld_n[min(di + 1, len(ld_n) - 1)]
          if di + 1 < len(DT) else ld_n[di])
        s1t = ds1[0:p, s1c0:s1c0 + nseg]
        for g in range(nseg):
            nlast = v_op(lambda e, p_=p, b=buf, g_=g:
                         e.reduce_max(ds1f[0:p_, g_:g_ + 1],
                                      dwt[0:p_, 2048 * b + 64 * g_:
                                          2048 * b + 64 * g_ + 64],
                                      axis=AX.X, apply_absolute_value=True),
                         inc=(g == nseg - 1))
        W(P.v, ve, nlast)
        v_seq(lambda e, s=s1t, p_=p, ns=nseg:
              e.tensor_copy(s, ds1f[0:p_, 0:ns]))
        v_seq(lambda e, s=s1t, p_=p: e.reduce_sum(sA[0:p_, 0:1], s,
                                                  axis=AX.X))
        nv = v_seq(lambda e, p_=p, w_=w:
                   e.tensor_tensor(psc[0:p_, w_:w_ + 1], psc[0:p_, w_:w_ + 1],
                                   sA[0:p_, 0:1], ALU.add))
        vred.append(nv)
        if di + 2 < len(DT):
            _emit_passA_load(di + 2)
        W(P.s, ve, nv)
        dma(P.s, q_sc, s1_h[w].rearrange("(r c) -> r c", c=segw)
            [r0:r0 + p, c0 // 64:c0 // 64 + nseg], s1t)
    g_s1done = ("q_sc", fence(P.s, q_sc))

    W(P.t, ve, vred[-1])
    W(P.t, q_in, n_const)
    for w in range(3):
        t_mm(PS[0][0:1, 0:1], onesf[:], psc[:, w:w + 1], True, True, inc=True)
        n_pe = pe.n
        W(P.v, pe, n_pe)
        nvc = v_op(lambda e, w_=w: e.tensor_copy(psc[0:1, 4 + w_:5 + w_],
                                                 PS[0][0:1, 0:1]), inc=True)
        W(P.t, ve, nvc)
    n_part = ve.n
    W(P.g, ve, n_part)
    g_agp = dma(P.g, q_g, agp_in[:], psc[0:1, 4:12])
    g_agp = fence(P.g, q_g)
    W(P.g, q_g, g_agp)
    cc_agp = collective(agp_in.opt(), agp_out.opt())
    W(P.y, cc, cc_agp)
    n_agp = dma(P.y, q_in, sA[0:8, 0:8], agp_out[:])
    n_agp = fence(P.y, q_in)
    W(P.t, q_in, n_agp)
    t_mm(PS[0][0:1, 0:8], onesf[0:8, 0:1], sA[0:8, 0:8], True, True,
         inc=True)
    W(P.v, pe, pe.n)
    v_seq(lambda e: e.tensor_copy(psc[0:1, 8:11], PS[0][0:1, 0:3]))
    v_seq(lambda e: e.tensor_single_scalar(psc[0:1, 8:11], psc[0:1, 8:11],
                                           1.0 / NBLK, ALU.mult))
    v_seq(lambda e: e.tensor_copy(dsc88[0:1, 1024:1027], psc[0:1, 8:11]))
    nv_mean = v_seq(lambda e: e.tensor_copy(psc[0:1, 12:15],
                                            dsc88[0:1, 1024:1027]))
    W(P.t, ve, nv_mean)
    for w in range(3):
        t_mm(PS[1][:, 0:1], onesrow[:], psc[0:1, 12 + w:13 + w], True, True,
             inc=True)
        W(P.v, pe, pe.n)
        nvc = v_op(lambda e, w_=w: e.tensor_copy(psc[:, 4 + w_:5 + w_],
                                                 PS[1][:, 0:1]), inc=True)
        W(P.t, ve, nvc)
    nv_mrep = ve.n

    W(P.y, q_sc, g_s1done[1])
    prev_ds_store = None
    for w in range(3):
        W(P.y, q_g, prev_ds_store)
        n_s88 = dma(P.y, q_in, dsc88[0:88, 0:256],
                    s1_h[w].rearrange("(r c) -> r c", c=256))
        W(P.v, q_in, n_s88)
        v_seq(lambda e, w_=w: e.tensor_single_scalar(
            dsc88[0:88, 256:512], dsc88[0:88, 0:256],
            psc[0:88, 4 + w_:5 + w_], ALU.subtract))
        v_seq(lambda e: e.reduce_max(dsc88f[0:88, 0:1],
                                    dsc88[0:88, 256:512], axis=AX.X,
                                    apply_absolute_value=True))
        v_seq(lambda e: e.reciprocal(dsc88f[0:88, 1:2], dsc88f[0:88, 0:1]))
        v_seq(lambda e: e.tensor_single_scalar(dsc88[0:88, 514:515],
                                              dsc88f[0:88, 1:2], 128.0,
                                              ALU.mult))
        v_seq(lambda e: e.tensor_copy(dsc88f[0:88, 2:3], dsc88[0:88, 514:515]))
        v_seq(lambda e: e.reciprocal(dsc88f[0:88, 3:4], dsc88f[0:88, 2:3]))
        v_seq(lambda e: e.tensor_single_scalar(
            dsc88[0:88, 768:1024], dsc88[0:88, 256:512], dsc88f[0:88, 2:3],
            ALU.mult))
        v_seq(lambda e: e.tensor_scalar(dsc88f[0:88, 256:512],
                                       dsc88[0:88, 768:1024], RND_C, RND_C,
                                       ALU.add, ALU.subtract))
        v_seq(lambda e: e.tensor_scalar(dsc88f[0:88, 512:768],
                                       dsc88f[0:88, 256:512], -128.0, 127.0,
                                       ALU.max, ALU.min))
        v_seq(lambda e: e.tensor_single_scalar(dsc88[0:88, 768:1024],
                                              dsc88f[0:88, 512:768],
                                              dsc88f[0:88, 3:4], ALU.mult))
        nv = v_seq(lambda e, w_=w: e.tensor_single_scalar(
            dsc88[0:88, 0:256], dsc88[0:88, 768:1024],
            psc[0:88, 4 + w_:5 + w_], ALU.add))
        W(P.g, ve, nv)
        dma(P.g, q_g, ds_h[w].rearrange("(r c) -> r c", c=256),
            dsc88[0:88, 0:256])
        prev_ds_store = fence(P.g, q_g)
        if dbg:
            dma(P.g, q_g, dbg_out["d88f"][w], dsc88f[0:88, :])
            prev_ds_store = dma(P.g, q_g, dbg_out["d88b"][w],
                                dsc88[0:88, :])
    g_dsdone = q_g.n

    barrier()
    if ph <= 0:
        finish_early()
        return nc

    W(P.y, q_g, g_w)
    n_xb = dma(P.y, q_in, U[:, 0:32768].rearrange("p (i c) -> p i c", c=2048),
               xbf_h[:].rearrange("(i p) c -> p i c", p=128))
    n_xb = fence(P.y, q_in)

    passB = {"wp_ready": [None] * len(DT), "tr_done": [None] * len(DT),
             "stg_stored": [None] * len(DT)}
    tr_ev_ring = []
    last_di_of_w = {w: max(di for di in range(len(DT)) if DT[di][0] == w)
                    for w in range(3)}
    cc_agd = {}
    cc_tail = [None]

    def emit_passB_tile(di):
        w, wd, r0, p, c0, ncols, segw, s1c0 = DT[di]
        nseg = ncols // 64
        buf = di % 2
        if di >= 2:
            W(P.y, ve, passB["wp_ready"][di - 2])
        n_ld = dma(P.y, q_in, dwt[0:p, 2048 * buf:2048 * buf + ncols],
                   wd[r0:r0 + p, c0:c0 + ncols])
        n_ds = dma(P.y, q_in, dst2[0:p, 32 * buf:32 * buf + nseg],
                   ds_h[w].rearrange("(r c) -> r c", c=segw)
                   [r0:r0 + p, c0 // 64:c0 // 64 + nseg])
        fence(P.y, q_in)
        n_ld = n_ds + 16
        s1t = ds1[0:p, s1c0:s1c0 + nseg]
        v_seq(lambda e, s=s1t, p_=p, ns=nseg:
              e.tensor_copy(ds1f[0:p_, 0:ns], s))
        v_seq(lambda e, p_=p, ns=nseg: e.reciprocal(dsrec[0:p_, 0:ns],
                                                    ds1f[0:p_, 0:ns]))
        W(P.v, q_in, n_ld)
        W(P.v, ve, ve.n)
        for g in range(nseg):
            nlast = v_op(lambda e, p_=p, b=buf, g_=g: e.tensor_single_scalar(
                dplane[0:p_, 64 * g_:64 * g_ + 64],
                dwt[0:p_, 2048 * b + 64 * g_:2048 * b + 64 * g_ + 64],
                dsrec[0:p_, g_:g_ + 1], ALU.mult), inc=(g == nseg - 1))
        W(P.v, ve, nlast)
        v_op(lambda e, p_=p, nc_=ncols: e.memset(dacc[0:p_, 0:nc_], -1.0))
        for j in range(15):
            v_op(lambda e, p_=p, nc_=ncols, b=buf, tj=NF4_T[j], dj=NF4_D[j]:
                 e.tensor_scalar(dwt[0:p_, 2048 * b:2048 * b + nc_],
                                 dplane[0:p_, 0:nc_], tj, dj,
                                 ALU.is_ge, ALU.mult))
            v_op(lambda e, p_=p, nc_=ncols, b=buf: e.tensor_tensor(
                dacc[0:p_, 0:nc_], dacc[0:p_, 0:nc_],
                dwt[0:p_, 2048 * b:2048 * b + nc_], ALU.add))
        if di >= 1:
            W(P.v, pe, passB["tr_done"][di - 1])
        W(P.v, q_in, n_ds + 16)
        v_seq(lambda e, p_=p, ns=nseg, b=buf: e.tensor_copy(
            dstf[0:p_, 0:ns], dst2[0:p_, 32 * b:32 * b + ns]))
        W(P.v, ve, ve.n)
        nwp = None
        for g in range(nseg):
            nwp = v_op(lambda e, p_=p, g_=g: e.tensor_single_scalar(
                dwp[0:p_, 64 * g_:64 * g_ + 64],
                dacc[0:p_, 64 * g_:64 * g_ + 64],
                dstf[0:p_, g_:g_ + 1], ALU.mult), inc=(g == nseg - 1))
        passB["wp_ready"][di] = nwp

    def emit_transpose_batch(di):
        w, wd, r0, p, c0, ncols, segw, s1c0 = DT[di]
        nch = ncols // 128
        W(P.t, ve, passB["wp_ready"][di])
        prev_store = passB["stg_stored"][di - 1] if di >= 1 else None
        for ci in range(nch):
            bank = 5 + ci % 3
            if len(tr_ev_ring) >= 3:
                W(P.t, ve, tr_ev_ring[-3])
            n_tr = pe.inc(1)
            P.t(lambda e, p_=p, ci_=ci, bk=bank:
                e.transpose(PS[bk][:, 0:p_],
                            dwp[0:p_, 128 * ci_:128 * (ci_ + 1)],
                            eyef[0:p_, 0:p_]).then_inc(pe.h, 1))
            W(P.v, pe, n_tr)
            if ci == 0 and prev_store:
                W(P.v, q_g, prev_store)
            nev = v_op(lambda e, p_=p, ci_=ci, bk=bank: e.tensor_copy(
                dstg[:, p_ * ci_:p_ * (ci_ + 1)],
                PS[bk][:, 0:p_]), inc=True)
            tr_ev_ring.append(nev)
        passB["tr_done"][di] = pe.n
        W(P.g, ve, tr_ev_ring[-1])
        if w < 2:
            dest = agd_in[w].rearrange("(i p f) -> p i f", i=16, p=128)[
                :, :, r0:r0 + p]
            src = dstg[:, 0:16 * p].rearrange("p (i f) -> p i f", f=p)
        else:
            j0 = c0 // 128
            dest = agd_in[2].rearrange("(j p o) -> p j o", j=44, p=128)[
                :, j0:j0 + nch, r0:r0 + p]
            src = dstg[:, 0:nch * p].rearrange("p (j o) -> p j o", o=p)
        passB["stg_stored"][di] = dma(P.g, q_g, dest, src)

    W(P.t, q_in, n_xb)
    W(P.t, q_g, g_small)

    n_acts = None
    for tq in range(4):
        for i in range(16):
            t_mm(PS[4][0:80, :], ballt_sb[:, 80 * i:80 * (i + 1)],
                 xv(i)[:, 512 * tq:512 * (tq + 1)], i == 0, i == 15,
                 inc=(i == 15))
        n_pe = pe.n
        W(P.v, pe, n_pe)
        n_acts = v_op(lambda e, tq_=tq: e.tensor_single_scalar(
            acts[0:80, 512 * tq_:512 * (tq_ + 1)], PS[4][0:80, :], 0.5,
            ALU.mult), inc=True)
        if tq < 3:
            W(P.t, ve, n_acts)

    def load_wcol(put, wsel, j, buf):
        dstap = (wcolA if buf == 0 else wcolB)[:, 0:2048].rearrange(
            "p (i c) -> p i c", c=128)
        srcap = agw_out[wsel].rearrange("(i p) c -> p i c", p=128)[
            :, :, 128 * j:128 * (j + 1)]
        return dma(put, q_in, dstap, srcap)

    W(P.y, cc, agw_n[0])
    unit_no = [0]
    trq = list(range(len(DT)))
    emitted_B = [0]

    def maybe_interleave():
        u = unit_no[0]
        unit_no[0] += 1
        if p1sub < 5:
            return
        want_B = min(len(DT), 2 + u // 8)
        while emitted_B[0] < want_B:
            di = emitted_B[0]
            if di >= 1 and trq and trq[0] == di - 1:
                emit_transpose_batch(trq.pop(0))
            emit_passB_tile(di)
            emitted_B[0] += 1

    psum_user = [None] * 8
    stg_store = {}
    wcol_pe = {}

    def qk_units(wsel, kind, tqs, dram_dst, bcol0, prev_kind):
        for j in range(16):
            buf = j % 2
            if j == 0:
                W(P.y, cc, agw_n[wsel] + 1)
            if j >= 2:
                W(P.y, pe, wcol_pe[(kind, j - 2)])
            elif prev_kind is not None:
                W(P.y, pe, wcol_pe[(prev_kind, 14 + j)])
            n_w = load_wcol(P.y, wsel, j, buf)
            if j == 15:
                fence(P.y, q_in)
            for tq in tqs:
                u = unit_no[0]
                ps = u % 4
                if psum_user[ps] is not None:
                    W(P.t, psum_user[ps][0], psum_user[ps][1])
                W(P.t, q_in, n_w + 16)
                wc = wcolA if buf == 0 else wcolB
                for i in range(16):
                    t_mm(PS[ps][:, :], wc[:, 128 * i:128 * (i + 1)],
                         xv(i)[:, 512 * tq:512 * (tq + 1)], i == 0, False)
                arow = 0 if kind == "q" else 32
                npe = t_mm(PS[ps][:, :],
                           at4_sb[arow:arow + 16, 128 * j:128 * j + 128],
                           acts[arow:arow + 16, 512 * tq:512 * (tq + 1)],
                           False, True, inc=True)
                wcol_pe[(kind, j)] = npe
                W(P.s, pe, npe)
                W(P.s, q_sc, stg_store.get(u - 3))
                stg = pt3[:, 512 * (u % 3):512 * (u % 3) + 512]
                nse = s_op(lambda e, ps_=ps, st=stg, bj=bcol0 + j:
                           e.activation(st, PS[ps_][:, :], AF.Identity,
                                        bias=bcols_sb[:, bj:bj + 1]),
                           inc=True)
                psum_user[ps] = (se, nse)
                W(P.s, se, nse)
                stg_store[u] = dma(P.s, q_sc,
                                   dram_dst[128 * j:128 * (j + 1),
                                            512 * tq:512 * (tq + 1)], stg)
                maybe_interleave()

    if p1sub >= 2:
        qk_units(0, "q", (0, 1), q_hbm, 0, None)
    if p1sub >= 3:
        qk_units(1, "k", (0, 1, 2, 3), k_hbm, 16, "q")
    if p1sub < 4:
        barrier()
        finish_early()
        return nc

    vt_pe_last = [wcol_pe[("k", 15)]]

    for oq in range(4):
        if oq == 0:
            W(P.y, cc, agw_n[2] + 1)
        W(P.y, pe, vt_pe_last[0])
        dma(P.y, q_in,
            wcolA[:].rearrange("p (i c) -> p i c", c=512),
            agw_out[2][0:1024, :].rearrange("(i p) c -> p i c", p=128)[
                :, :, 512 * oq:512 * (oq + 1)])
        nldB = dma(P.y, q_in,
                   wcolB[:].rearrange("p (i c) -> p i c", c=512),
                   agw_out[2][1024:2048, :].rearrange("(i p) c -> p i c",
                                                      p=128)[
                       :, :, 512 * oq:512 * (oq + 1)])
        fence(P.y, q_in)
        for tc in range(16):
            u = unit_no[0]
            ps = u % 4
            if psum_user[ps] is not None:
                W(P.t, psum_user[ps][0], psum_user[ps][1])
            W(P.t, q_in, nldB + 16)
            for i in range(16):
                wc = wcolA if i < 8 else wcolB
                ii = i % 8
                t_mm(PS[ps][:, :], xv(i)[:, 128 * tc:128 * (tc + 1)],
                     wc[:, 512 * ii:512 * (ii + 1)], i == 0, False)
            npe = t_mm(PS[ps][:, :], acts[64:80, 128 * tc:128 * (tc + 1)],
                       at4_sb[64:80, 512 * oq:512 * oq + 512],
                       False, True, inc=True)
            vt_pe_last[0] = npe
            W(P.v, pe, npe)
            W(P.v, q_in, n_bvrep)
            W(P.v, q_sc, stg_store.get(u - 3))
            stg = pt3[:, 512 * (u % 3):512 * (u % 3) + 512]
            nve = v_op(lambda e, ps_=ps, st=stg, o_=oq: e.tensor_tensor(
                st, PS[ps_][:, :],
                bvrep_sb[:, 512 * o_:512 * (o_ + 1)], ALU.add), inc=True)
            psum_user[ps] = (ve, nve)
            W(P.s, ve, nve)
            stg_store[u] = dma(P.s, q_sc,
                               vt_hbm[128 * tc:128 * (tc + 1),
                                      512 * oq:512 * (oq + 1)], stg)
            maybe_interleave()

    if p1sub < 5:
        barrier()
        finish_early()
        return nc
    while emitted_B[0] < len(DT):
        di = emitted_B[0]
        if di >= 1 and trq and trq[0] == di - 1:
            emit_transpose_batch(trq.pop(0))
        emit_passB_tile(di)
        emitted_B[0] += 1
    while trq:
        emit_transpose_batch(trq.pop(0))

    for w in range(3):
        W(P.g, q_g, passB["stg_stored"][last_di_of_w[w]])
    W(P.g, q_g, fence(P.g, q_g))
    for w in range(3):
        cc_agd[w] = collective(agd_in[w].opt(), agd_out[w].opt())
    P.g(lambda e, v=cc_agd[2]: e.wait_ge(cc.h, v))

    f2_in, f2_g, f2_sc = fence(P.y, q_in), fence(P.g, q_g), fence(P.s, q_sc)
    barrier()
    if ph <= 1:
        finish_early()
        return nc

    W(P.y, q_sc, f2_sc)
    unit_pe = {}
    exp_se = {}
    av_done_v = [None]
    sm_done_s = [None]
    att_store = {}
    sum_store = {}
    pt_user = [None, None]
    kv_pe = {}
    score_pe = {}
    exp_pair_se = {}
    pt4 = U[:, 8192:10240]

    for u in range(32):
        h, f = u // 2, u % 2
        kbuf = h % 2
        if f == 0:
            if h >= 2:
                W(P.y, pe, kv_pe.get(h - 2))
            dma(P.y, q_in, kh2[:, 2048 * kbuf:2048 * (kbuf + 1)],
                k_hbm[128 * h:128 * (h + 1), :])
            dma(P.y, q_in, vt2[:, 2048 * kbuf:2048 * (kbuf + 1)]
                .rearrange("p (t c) -> p t c", c=128),
                vt_hbm[:, 128 * h:128 * (h + 1)]
                .rearrange("(t p) c -> p t c", p=128))
        qbuf = u % 2
        W(P.y, pe, unit_pe.get(u - 2))
        n_q = dma(P.y, q_in, qh2[:, 512 * qbuf:512 * (qbuf + 1)],
                  q_hbm[128 * h:128 * (h + 1), 512 * f:512 * (f + 1)])
        W(P.t, q_in, n_q + 32)

        def score2(kt):
            gkt = 16 * u + kt
            score_pe[gkt] = t_mm(
                P2B[(kt // 2) % 2][:, 512 * (kt % 2):512 * (kt % 2) + 512],
                kh2[:, 2048 * kbuf + 128 * kt:2048 * kbuf + 128 * kt + 128],
                qh2[:, 512 * qbuf:512 * (qbuf + 1)], True, True, inc=True)

        for p in (0, 1):
            W(P.t, se, exp_pair_se.get(8 * u + p - 2))
            score2(2 * p)
            score2(2 * p + 1)
        W(P.t, ve, av_done_v[0])
        W(P.t, se, sm_done_s[0])
        W(P.v, q_g, att_store.get(u - 2))
        W(P.s, q_sc, sum_store.get(u - 2))
        for pr in range(8):
            gpr = 8 * u + pr
            pos = gpr % 2
            W(P.s, pe, score_pe[16 * u + 2 * pr + 1])
            W(P.s, pe, pt_user[pos])
            nse = s_op(lambda e, pb_=pr % 2, pos_=pos: e.activation(
                pt4[:, 1024 * pos_:1024 * (pos_ + 1)], P2B[pb_][:, :],
                AF.Exp, bias=psc[:, 15:16], scale=SCALE), inc=True)
            exp_pair_se[gpr] = nse
            W(P.t, se, nse)
            npe2 = None
            for t in (0, 1):
                kt = 2 * pr + t
                e_sl = pt4[:, 1024 * pos + 512 * t:1024 * pos + 512 * t + 512]
                t_mm(PS[4][:, :],
                     vt2[:, 2048 * kbuf + 128 * kt:
                         2048 * kbuf + 128 * kt + 128],
                     e_sl, kt == 0, kt == 15)
                npe2 = t_mm(PS[5][0:1, :], onesb[:, :], e_sl, kt == 0,
                            kt == 15, inc=True)
            pt_user[pos] = npe2
            if pr + 2 <= 7:
                score2(2 * (pr + 2))
                score2(2 * (pr + 2) + 1)
        unit_pe[u] = pe.n
        kv_pe[h] = pe.n
        W(P.v, pe, pe.n)
        nev = v_op(lambda e, u_=u: e.tensor_copy(
            sA[:, 512 * (u_ % 2):512 * (u_ % 2) + 512], PS[4][:, :]),
            inc=True)
        av_done_v[0] = nev
        W(P.g, ve, nev)
        att_store[u] = dma(P.g, q_g,
                           att_h[128 * h:128 * (h + 1), 512 * f:512 * (f + 1)],
                           sA[:, 512 * (u % 2):512 * (u % 2) + 512])
        W(P.s, pe, pe.n)
        nss = s_op(lambda e, u_=u: e.copy(
            lnrow[0:1, 512 * (u_ % 2):512 * (u_ % 2) + 512],
            PS[5][0:1, :]), inc=True)
        sm_done_s[0] = nss
        W(P.s, se, nss)
        sum_store[u] = dma(P.s, q_sc,
                           sums_h[h, 512 * f:512 * (f + 1)].unsqueeze(0),
                           lnrow[0:1, 512 * (u % 2):512 * (u % 2) + 512])

    fence(P.y, q_in)
    f3_in, f3_g, f3_sc = fence(P.y, q_in), fence(P.g, q_g), fence(P.s, q_sc)
    barrier()
    if ph <= 2:
        finish_early()
        return nc

    W(P.y, q_sc, f3_sc)
    W(P.y, q_g, f3_g)
    n_sums = dma(P.y, q_in, sums16, sums_h[:])
    n_sums = fence(P.y, q_in)
    W(P.v, q_in, n_sums)
    nv_rec = v_op(lambda e: e.reciprocal(recip16, sums16), inc=True)
    W(P.g, ve, nv_rec)
    g_rcp = dma(P.g, q_g, rcp_h[:], recip16)
    g_rcp = fence(P.g, q_g)

    attb = U
    n_attb_rep_pe = [0]
    norm_ev = {}
    W(P.t, ve, nv_rec)
    for u in range(32):
        h, f = u // 2, u % 2
        W(P.y, ve, norm_ev.get(u - 2))
        nld = dma(P.y, q_in, sC[:, 512 * (u % 2):512 * (u % 2) + 512],
                  att_h[128 * h:128 * (h + 1), 512 * f:512 * (f + 1)])
        W(P.y, q_g, g_rcp)
        nldr = dma(P.y, q_in, lnrow[0:1, 512 * (u % 2):512 * (u % 2) + 512],
                   rcp_h[h, 512 * f:512 * (f + 1)].unsqueeze(0))
        if u == 31:
            nldr = fence(P.y, q_in)
        nld = nldr
        if u >= 2:
            W(P.t, ve, norm_ev.get(u - 2))
        W(P.t, q_in, nldr)
        npe = t_mm(PS[2 + u % 2][:, :], onesrow[:],
                   lnrow[0:1, 512 * (u % 2):512 * (u % 2) + 512], True, True,
                   inc=True)
        W(P.v, pe, npe)
        W(P.v, q_in, nld)
        nev = v_op(lambda e, u_=u, h_=h, f_=f: e.tensor_tensor(
            attb[:, 1024 * h_ + 512 * f_:1024 * h_ + 512 * f_ + 512],
            sC[:, 512 * (u_ % 2):512 * (u_ % 2) + 512],
            PS[2 + u_ % 2][:, :], ALU.mult), inc=True)
        norm_ev[u] = nev
    n_attb = norm_ev[31]

    W(P.t, ve, n_attb)
    n_actso = None
    for tq in range(2):
        for i in range(16):
            t_mm(PS[4][0:16, :], bot_sb[:, 16 * i:16 * (i + 1)],
                 attb[:, 1024 * i + 512 * tq:1024 * i + 512 * tq + 512],
                 i == 0, i == 15, inc=(i == 15))
        W(P.v, pe, pe.n)
        n_actso = v_op(lambda e, tq_=tq: e.tensor_single_scalar(
            acts_o[0:16, 512 * tq_:512 * (tq_ + 1)], PS[4][0:16, :], 0.5,
            ALU.mult), inc=True)
        if tq == 0:
            W(P.t, ve, n_actso)

    o_pe = {}
    o_ev = {}
    o_sq = {}
    o_st = {}
    o_stats_pe = {}
    W(P.t, ve, n_actso)
    for j in range(16):
        buf = j % 2
        if j == 0:
            W(P.y, cc, agw_n[3] + 1)
        if j >= 2:
            W(P.y, pe, o_stats_pe.get(2 * (j - 2) + 1, None))
        n_w = load_wcol(P.y, 3, j, buf)
        W(P.y, ve, o_ev.get(2 * (j - 2) + 1))
        if j < 2:
            W(P.y, ve, nv_rec)
            W(P.y, q_g, g_rcp)
            W(P.y, pe, n_attb_rep_pe[0])
        nldx = dma(P.y, q_in, rx2[:, 1024 * buf:1024 * buf + 1024],
                   xt[128 * j:128 * (j + 1), 0:1024])
        if j == 15:
            fence(P.y, q_in)
        for f in range(2):
            u = 2 * j + f
            ps = u % 2
            W(P.t, ve, o_ev.get(u - 2))
            W(P.t, q_in, n_w + 16)
            wc = wcolA if buf == 0 else wcolB
            for i in range(16):
                t_mm(PS[ps][:, :], wc[:, 128 * i:128 * (i + 1)],
                     attb[:, 1024 * i + 512 * f:1024 * i + 512 * f + 512],
                     i == 0, False)
            npe = t_mm(PS[ps][:, :],
                       at4_sb[0:16, 2048 + 128 * j:2048 + 128 * j + 128],
                       acts_o[0:16, 512 * f:512 * (f + 1)], False, True,
                       inc=True)
            o_pe[u] = npe
            W(P.v, pe, npe)
            W(P.v, q_in, nldx + 16)
            W(P.v, q_g, o_st.get(u - 2))
            W(P.v, se, o_sq.get(u - 2))
            W(P.v, pe, o_stats_pe.get(u - 2))
            nev = v_op(lambda e, ps_=ps, j_=j, f_=f, b=buf:
                       e.scalar_tensor_tensor(
                           sA[:, 512 * ps_:512 * ps_ + 512],
                           PS[ps_][:, :], bcols_sb[:, 32 + j_:33 + j_],
                           rx2[:, 1024 * b + 512 * f_:
                               1024 * b + 512 * f_ + 512],
                           ALU.add, ALU.add), inc=True)
            o_ev[u] = nev
            W(P.g, ve, nev)
            o_st[u] = dma(P.g, q_g,
                          r_h[128 * j:128 * (j + 1), 512 * f:512 * (f + 1)],
                          sA[:, 512 * ps:512 * ps + 512])
            W(P.s, ve, nev)
            nsq = s_op(lambda e, ps_=ps: e.activation(
                sB[:, 512 * ps_:512 * ps_ + 512],
                sA[:, 512 * ps_:512 * ps_ + 512], AF.Square), inc=True)
            o_sq[u] = nsq
            W(P.t, ve, nev)
            W(P.t, se, nsq)
            t_mm(PS[4 + f][0:1, :], onesf[:], sA[:, 512 * ps:512 * ps + 512],
                 j == 0, j == 15)
            npe3 = t_mm(PS[6 + f][0:1, :], onesf[:],
                        sB[:, 512 * ps:512 * ps + 512], j == 0, j == 15,
                        inc=True)
            o_stats_pe[u] = npe3

    def ln_tail():
        W(P.v, pe, pe.n)
        for i, ps in enumerate((4, 5, 6, 7)):
            v_op(lambda e, i_=i, ps_=ps: e.tensor_copy(
                lnrow[0:1, 512 * i_:512 * (i_ + 1)], PS[ps_][0:1, :]))
        v_seq(lambda e: e.tensor_single_scalar(
            lnrow[0:1, 0:1024], lnrow[0:1, 0:1024], 1.0 / D, ALU.mult))
        v_seq(lambda e: e.tensor_single_scalar(
            lnrow[0:1, 1024:2048], lnrow[0:1, 1024:2048], 1.0 / D, ALU.mult))
        v_seq(lambda e: e.tensor_tensor(
            sC[0:1, 0:1024], lnrow[0:1, 0:1024], lnrow[0:1, 0:1024],
            ALU.mult))
        v_seq(lambda e: e.tensor_tensor(
            lnrow[0:1, 1024:2048], lnrow[0:1, 1024:2048], sC[0:1, 0:1024],
            ALU.subtract))
        nv1 = v_seq(lambda e: e.tensor_single_scalar(
            lnrow[0:1, 1024:2048], lnrow[0:1, 1024:2048], EPS, ALU.add))
        W(P.s, ve, nv1)
        ns1 = s_op(lambda e: e.activation(sC[0:1, 0:1024],
                                          lnrow[0:1, 1024:2048], AF.Sqrt),
                   inc=True)
        W(P.v, se, ns1)
        nv2 = v_op(lambda e: e.reciprocal(lnrow[0:1, 1024:2048],
                                          sC[0:1, 0:1024]), inc=True)
        W(P.t, ve, nv2)
        for f in range(2):
            t_mm(PS[4 + f][:, :], onesrow[:],
                 lnrow[0:1, 512 * f:512 * (f + 1)], True, True)
        n = None
        for f in range(2):
            n = t_mm(PS[6 + f][:, :], onesrow[:],
                     lnrow[0:1, 1024 + 512 * f:1024 + 512 * (f + 1)],
                     True, True, inc=(f == 1))
        return n

    n_rep1 = ln_tail()

    f_rst = fence(P.g, q_g)
    nrm_ev = {}
    nrm_st = {}
    for i in range(16):
        buf = i % 2
        W(P.y, ve, nrm_ev.get(i - 2))
        if i < 2:
            W(P.y, ve, o_ev.get(2 * (14 + i) + 1))
        W(P.y, q_g, f_rst)
        nld = dma(P.y, q_in, rx2[:, 1024 * buf:1024 * buf + 1024],
                  r_h[128 * i:128 * (i + 1), :])
        if i == 15:
            nld = fence(P.y, q_in)
        for f in range(2):
            W(P.v, pe, n_rep1)
            W(P.v, q_in, nld + 16 if i < 15 else nld)
            W(P.v, q_g, nrm_st.get((i - 1, f)))
            v_op(lambda e, b=buf, f_=f: e.tensor_tensor(
                sB[:, 512 * f_:512 * f_ + 512],
                rx2[:, 1024 * b + 512 * f_:1024 * b + 512 * f_ + 512],
                PS[4 + f_][:, :], ALU.subtract))
            v_op(lambda e, f_=f: e.tensor_tensor(
                sA[:, 512 * f_:512 * f_ + 512], sB[:, 512 * f_:512 * f_ + 512],
                PS[6 + f_][:, :], ALU.mult))
            nev = v_op(lambda e, i_=i, f_=f: e.tensor_copy(
                x2b[:, 1024 * i_ + 512 * f_:1024 * i_ + 512 * f_ + 512],
                sA[:, 512 * f_:512 * f_ + 512]), inc=True)
            nrm_ev[i] = nev
            W(P.g, ve, nev)
            nrm_st[(i, f)] = dma(P.g, q_g,
                                 x2f_h[128 * i:128 * (i + 1),
                                       512 * f:512 * (f + 1)],
                                 sA[:, 512 * f:512 * f + 512])

    f4_in, f4_g, f4_sc = fence(P.y, q_in), fence(P.g, q_g), fence(P.s, q_sc)
    barrier()
    if ph <= 3:
        finish_early()
        return nc

    W(P.t, cc, cc_agd[2])
    W(P.y, cc, cc_agd[2])
    W(P.s, cc, cc_agd[2])
    W(P.v, cc, cc_agd[2])

    def load_fcol(put, q, w, j, dstbuf, i0, i1):
        f0, f1 = 128 * j, 128 * (j + 1)
        r0, r1 = f0 // 704, (f1 - 1) // 704
        n = None
        for r in range(r0, r1 + 1):
            lo = max(f0, 704 * r)
            hi = min(f1, 704 * (r + 1))
            src = agd_out[w][r].rearrange("(i p f) -> p i f", i=16, p=128)[
                :, i0:i1, lo - 704 * r:hi - 704 * r]
            dst = dstbuf.rearrange("p (i c) -> p i c", c=128)[
                :, i0:i1, lo - f0:hi - f0]
            n = dma(put, q, dst, src)
        return n

    hv = U
    ab_user = [None] * 8
    ab_pe = {}
    ab_h = {}
    for j in range(44):
        buf = j % 2
        g_ab = ab_pe.get(j - 2)
        W(P.y, pe, g_ab)
        W(P.s, pe, g_ab)
        W(P.g, pe, g_ab)
        wc1buf = (wcolA if buf == 0 else wcolB)[:, 0:2048]
        w2c = rx2[:, 0:1024].bitcast(BF) if buf == 0 \
            else rx2[:, 1024:2048].bitcast(BF)
        load_fcol(P.y, q_in, 0, j, wc1buf, 0, 11)
        f_w1a = fence(P.y, q_in)
        load_fcol(P.s, q_sc, 0, j, wc1buf, 11, 16)
        load_fcol(P.s, q_sc, 1, j, w2c, 0, 6)
        f_w1b = fence(P.s, q_sc)
        load_fcol(P.g, q_g, 1, j, w2c, 6, 16)
        f_w2a = fence(P.g, q_g)
        for f in range(2):
            pa = 4 * (j % 2) + f
            pb = 4 * (j % 2) + 2 + f
            W(P.t, ve, ab_user[pa])
            W(P.t, ve, ab_user[pb])
            W(P.t, q_in, f_w1a)
            W(P.t, q_sc, f_w1b)
            W(P.t, q_g, f_w2a)
            wc1 = (wcolA if buf == 0 else wcolB)[:]
            wc2 = rx2[:, 0:1024].bitcast(BF) if buf == 0 \
                else rx2[:, 1024:2048].bitcast(BF)
            for i in range(16):
                t_mm(PS[pa][:, :], wc1[:, 128 * i:128 * (i + 1)],
                     x2b[:, 1024 * i + 512 * f:1024 * i + 512 * f + 512],
                     i == 0, i == 15)
            for i in range(16):
                t_mm(PS[pb][:, :], wc2[:, 128 * i:128 * (i + 1)],
                     x2b[:, 1024 * i + 512 * f:1024 * i + 512 * f + 512],
                     i == 0, i == 15, inc=(i == 15))
            npe = pe.n
            ab_pe[j] = npe
            W(P.s, pe, npe)
            W(P.s, ve, ab_h.get((j - 1, f)))
            nsi = s_op(lambda e, pa_=pa, f_=f: e.activation(
                sA[:, 512 * f_:512 * f_ + 512], PS[pa_][:, :], AF.Silu),
                inc=True)
            W(P.v, se, nsi)
            nev = v_op(lambda e, pb_=pb, j_=j, f_=f: e.tensor_tensor(
                hv[:, 1024 * j_ + 512 * f_:1024 * j_ + 512 * f_ + 512],
                sA[:, 512 * f_:512 * f_ + 512], PS[pb_][:, :], ALU.mult),
                inc=True)
            ab_user[pa] = nev
            ab_user[pb] = nev
            ab_h[(j, f)] = nev

    f5_in, f5_g, f5_sc = fence(P.y, q_in), fence(P.g, q_g), fence(P.s, q_sc)
    barrier()
    if ph <= 4:
        finish_early()
        return nc

    W(P.t, cc, cc_agd[2])
    W(P.y, cc, cc_agd[2])
    W(P.y, q_g, f4_g)
    w3_pe = {}
    w3_ev = {}
    w3_sq = {}
    w3_st = {}
    w3_stats_pe = {}
    w3g_n = {}
    for k in range(16):
        buf = k % 2
        if k >= 2:
            W(P.y, pe, w3_stats_pe.get(2 * (k - 2) + 1))
        dst = x2b[:, 5632 * buf:5632 * (buf + 1)].rearrange(
            "p (j c) -> p j c", c=128)
        src = agd_out[2][k // 2].rearrange("(j p o) -> p j o", j=44, p=128)[
            :, :, 128 * (k % 2):128 * (k % 2) + 128]
        g_w3 = w3_stats_pe.get(2 * (k - 2) + 1)
        W(P.s, pe, g_w3)
        W(P.g, pe, g_w3)
        dma(P.y, q_in, dst[:, 0:15, :], src[:, 0:15, :])
        f_w3a = fence(P.y, q_in)
        dma(P.s, q_sc, dst[:, 15:30, :], src[:, 15:30, :])
        f_w3b = fence(P.s, q_sc)
        dma(P.g, q_g, dst[:, 30:44, :], src[:, 30:44, :])
        f_w3c = fence(P.g, q_g)
        W(P.y, ve, w3_ev.get(2 * (k - 2) + 1))
        nldx = dma(P.y, q_in, rx2[:, 1024 * buf:1024 * buf + 1024],
                   x2f_h[128 * k:128 * (k + 1), :])
        if k == 15:
            fence(P.y, q_in)
        for f in range(2):
            u = 2 * k + f
            ps = 2 * (k % 2) + f
            W(P.t, ve, w3_ev.get(u - 4))
            W(P.t, q_in, f_w3a)
            W(P.t, q_sc, f_w3b)
            W(P.t, q_g, f_w3c)
            wc = x2b[:, 5632 * buf:5632 * (buf + 1)]
            for j in range(44):
                t_mm(PS[ps][:, :], wc[:, 128 * j:128 * (j + 1)],
                     hv[:, 1024 * j + 512 * f:1024 * j + 512 * f + 512],
                     j == 0, j == 43, inc=(j == 43))
            npe = pe.n
            w3_pe[u] = npe
            W(P.v, pe, npe)
            W(P.v, q_in, nldx + 16)
            W(P.v, q_g, w3_st.get(u - 2))
            W(P.v, se, w3_sq.get(u - 2))
            W(P.v, pe, w3_stats_pe.get(u - 2))
            nev = v_op(lambda e, ps_=ps, b=buf, f_=f: e.tensor_tensor(
                sA[:, 512 * f_:512 * f_ + 512], PS[ps_][:, :],
                rx2[:, 1024 * b + 512 * f_:1024 * b + 512 * f_ + 512],
                ALU.add), inc=True)
            w3_ev[u] = nev
            W(P.g, ve, nev)
            w3_st[u] = dma(P.g, q_g,
                           r2_h[128 * k:128 * (k + 1), 512 * f:512 * (f + 1)],
                           sA[:, 512 * f:512 * f + 512])
            W(P.s, ve, nev)
            nsq = s_op(lambda e, f_=f: e.activation(
                sB[:, 512 * f_:512 * f_ + 512],
                sA[:, 512 * f_:512 * f_ + 512], AF.Square), inc=True)
            w3_sq[u] = nsq
            W(P.t, ve, nev)
            W(P.t, se, nsq)
            t_mm(PS[4 + f][0:1, :], onesf[:], sA[:, 512 * f:512 * f + 512],
                 k == 0, k == 15)
            npe3 = t_mm(PS[6 + f][0:1, :], onesf[:],
                        sB[:, 512 * f:512 * f + 512], k == 0, k == 15,
                        inc=True)
            w3_stats_pe[u] = npe3

    n_rep2 = ln_tail()

    f_r2 = fence(P.g, q_g)
    out_ev = {}
    out_st = {}
    for k in range(16):
        buf = k % 2
        W(P.y, ve, out_ev.get(k - 2))
        if k < 2:
            W(P.y, ve, w3_ev.get(2 * (14 + k) + 1))
        W(P.y, q_g, f_r2)
        nld = dma(P.y, q_in, rx2[:, 1024 * buf:1024 * buf + 1024],
                  r2_h[128 * k:128 * (k + 1), :])
        fence(P.y, q_in)
        for f in range(2):
            W(P.v, pe, n_rep2)
            W(P.v, q_in, nld + 16)
            W(P.v, q_out, out_st.get((k - 1, f)))
            v_op(lambda e, b=buf, f_=f: e.tensor_tensor(
                sB[:, 512 * f_:512 * f_ + 512],
                rx2[:, 1024 * b + 512 * f_:1024 * b + 512 * f_ + 512],
                PS[4 + f_][:, :], ALU.subtract))
            nev = v_op(lambda e, f_=f: e.tensor_tensor(
                sA[:, 512 * f_:512 * f_ + 512],
                sB[:, 512 * f_:512 * f_ + 512],
                PS[6 + f_][:, :], ALU.mult), inc=True)
            out_ev[k] = nev
            W(P.y, ve, nev)
            out_st[(k, f)] = dma(P.y, q_out,
                                 out[128 * k:128 * (k + 1),
                                     512 * f:512 * (f + 1)],
                                 sA[:, 512 * f:512 * f + 512])
    dma(P.y, q_out, r_h[0:1, 0:1], dum[0:1, 0:1])
    P.y(lambda e: e.wait_ge(q_out.h, 16 * 33))
    if dbg:
        srcs = {"q_hbm": q_hbm, "k_hbm": k_hbm, "vt_hbm": vt_hbm,
                "att_h": att_h, "sums_h": sums_h, "r_h": r_h,
                "x2f_h": x2f_h, "r2_h": r2_h, "ds_h": ds_h,
                "agd0_in": agd_in[0], "agd2_in": agd_in[2], "s1_h": s1_h}
        W(P.g, q_out, 16 * 32)
        for nm, ap_ in srcs.items():
            dma(P.g, q_g, dbg_out[nm], ap_)
        dma(P.g, q_g, dbg_out["acts_d"], acts[0:80, :])
        nacts = dma(P.g, q_g, dbg_out["psc_d"], psc[:])
        P.g(lambda e, v=nacts: e.wait_ge(q_g.h, v))

    emit_block()
    return nc


def shard_inputs(inputs):
    f32 = np.float32
    x = np.ascontiguousarray(np.asarray(inputs["L0_attn_x"], dtype=f32))
    Wq, Wk = inputs["L0_attn_Wq"], inputs["L0_attn_Wk"]
    Wv, Wo = inputs["L0_attn_Wv"], inputs["L0_attn_Wo"]
    bq, bk = inputs["L0_attn_bq"], inputs["L0_attn_bk"]
    bv, bo = inputs["L0_attn_bv"], inputs["L0_attn_bo"]
    Bq, Bk = inputs["L0_attn_Bq"], inputs["L0_attn_Bk"]
    Bv, Bo = inputs["L0_attn_Bv"], inputs["L0_attn_Bo"]
    Aq, Ak = inputs["L0_attn_Aq"], inputs["L0_attn_Ak"]
    Av, Ao = inputs["L0_attn_Av"], inputs["L0_attn_Ao"]
    w1 = np.ascontiguousarray(np.asarray(inputs["L0_ffn_weight1"], dtype=BF16))
    w2 = np.ascontiguousarray(np.asarray(inputs["L0_ffn_weight2"], dtype=BF16))
    w3 = np.ascontiguousarray(np.asarray(inputs["L0_ffn_weight3"], dtype=BF16))

    WT = np.stack([np.ascontiguousarray(np.asarray(W, f32).T)
                   for W in (Wq, Wk, Wv, Wo)])
    bcols = np.zeros((128, 48), f32)
    for wi, b in enumerate((bq, bk, bo)):
        bcols[:, 16 * wi:16 * (wi + 1)] = np.asarray(b, f32).reshape(16, 128).T
    ballt = np.zeros((D, 80), f32)
    for gi, B in zip((0, 32, 64), (Bq, Bk, Bv)):
        ballt[:, gi:gi + 16] = np.asarray(B, f32).T
    ballt = np.ascontiguousarray(ballt)
    botm = np.ascontiguousarray(np.asarray(Bo, f32).T)
    at4m = np.zeros((80, 4096), f32)
    at4m[0:16, 0:2048] = np.asarray(Aq, f32).T
    at4m[0:16, 2048:4096] = np.asarray(Ao, f32).T
    at4m[32:48, 0:2048] = np.asarray(Ak, f32).T
    at4m[64:80, 0:2048] = np.asarray(Av, f32).T
    at4m = np.ascontiguousarray(at4m)
    eyefm = np.eye(128, dtype=f32)
    onesfm = np.ones((128, 1), f32)
    onesrowm = np.ones((1, 128), f32)
    bvrepm = np.ascontiguousarray(
        np.broadcast_to(np.asarray(bv, f32).reshape(1, D), (128, D)).copy())

    in_maps = []
    for c in range(8):
        b, hh = c // 2, c % 2
        xb = x[b]
        if hh == 1:
            xb = np.concatenate([xb[1024:], xb[:1024]], axis=0)
        m = {
            "xt": np.ascontiguousarray(xb.T),
            "wts": np.ascontiguousarray(WT[:, 256 * c:256 * (c + 1), :]),
            "bcols": bcols, "bvrep": bvrepm,
            "ballt": ballt, "bot": botm, "at4": at4m,
            "w1s": np.ascontiguousarray(w1[704 * c:704 * (c + 1)]),
            "w2s": np.ascontiguousarray(w2[704 * c:704 * (c + 1)]),
            "w3s": np.ascontiguousarray(w3[256 * c:256 * (c + 1)]),
            "eyef_d": eyefm, "onesf_d": onesfm, "onesrow_d": onesrowm,
        }
        in_maps.append(m)
    return in_maps


def kernel(**inputs):
    if "nc" not in _CACHED:
        _CACHED["nc"] = build_nc()
    nc = _CACHED["nc"]
    in_maps = shard_inputs(inputs)
    res = run_bass_kernel_spmd(nc, in_maps, core_ids=list(range(8)))
    full = np.zeros((4, TS, D), np.float32)
    for c in range(8):
        b, hh = c // 2, c % 2
        full[b, 1024 * hh:1024 * (hh + 1), :] = \
            np.asarray(res.results[c]["out"]).T
    return full


# version 18
# speedup vs baseline: 1.0177x; 1.0177x over previous
import numpy as np
import ml_dtypes

import concourse.bass as bass
import concourse.mybir as mybir
from concourse.bass_utils import run_bass_kernel_spmd

BF16 = ml_dtypes.bfloat16
F32 = mybir.dt.float32
BF = mybir.dt.bfloat16
ALU = mybir.AluOpType
AF = mybir.ActivationFunctionType
AX = mybir.AxisListType

NF4_T = [-0.84375, -0.60546875, -0.45703125, -0.337890625, -0.234375,
         -0.1376953125, -0.04541015625, 0.0400390625, 0.12060546875,
         0.2041015625, 0.29296875, 0.390625, 0.50390625, 0.64453125,
         0.86328125]
NF4_D = [0.3046875, 0.171875, 0.12890625, 0.109375, 0.1005859375,
         0.09326171875, 0.09130859375, 0.07958984375, 0.08154296875,
         0.0849609375, 0.091796875, 0.103515625, 0.12109375, 0.16015625,
         0.27734375]
RND_C = 12582912.0
C0 = 20.0
SCALE = float(128) ** -0.5
D = 2048
DFF = 5632
TMY = 1024
TS = 2048
H = 16
NBLK = DFF * D // 64
EPS = 1e-5
SLICE_EL = 2048 * 704

_CACHED = {}


class Ctr:
    __slots__ = ("h", "n")

    def __init__(self, h):
        self.h = h
        self.n = 0

    def inc(self, k=1):
        self.n += k
        return self.n


class Plan:
    ENGINES = ("tensor", "vector", "scalar", "sync", "gpsimd")

    def __init__(self):
        self.ops = {e: [] for e in self.ENGINES}

    def t(self, f):
        self.ops["tensor"].append(f)

    def v(self, f):
        self.ops["vector"].append(f)

    def s(self, f):
        self.ops["scalar"].append(f)

    def y(self, f):
        self.ops["sync"].append(f)

    def g(self, f):
        self.ops["gpsimd"].append(f)


def build_nc(ph=99, p1sub=99, dbg=False):
    nc = bass.Bass("TRN2", target_bir_lowering=False,
                   detect_race_conditions=False)

    xt = nc.dram_tensor("xt", [D, TS], F32, kind="ExternalInput").ap()
    wts = nc.dram_tensor("wts", [4, 256, D], F32, kind="ExternalInput").ap()
    bcols = nc.dram_tensor("bcols", [128, 48], F32, kind="ExternalInput").ap()
    bvrep = nc.dram_tensor("bvrep", [128, D], F32, kind="ExternalInput").ap()
    ballt = nc.dram_tensor("ballt", [D, 80], F32, kind="ExternalInput").ap()
    bot = nc.dram_tensor("bot", [D, 16], F32, kind="ExternalInput").ap()
    at4 = nc.dram_tensor("at4", [80, 4096], F32, kind="ExternalInput").ap()
    w1s = nc.dram_tensor("w1s", [704, D], BF, kind="ExternalInput").ap()
    w2s = nc.dram_tensor("w2s", [704, D], BF, kind="ExternalInput").ap()
    w3s = nc.dram_tensor("w3s", [256, DFF], BF, kind="ExternalInput").ap()
    eyef_d = nc.dram_tensor("eyef_d", [128, 128], F32, kind="ExternalInput").ap()
    onesf_d = nc.dram_tensor("onesf_d", [128, 1], F32, kind="ExternalInput").ap()
    onesrow_d = nc.dram_tensor("onesrow_d", [1, 128], F32,
                               kind="ExternalInput").ap()
    out = nc.dram_tensor("out", [D, TMY], F32, kind="ExternalOutput").ap()
    dbg_specs = [("q_hbm", [D, TMY], BF), ("k_hbm", [D, TS], BF),
                 ("vt_hbm", [TS, D], BF), ("att_h", [D, TMY], F32),
                 ("sums_h", [H, TMY], F32), ("r_h", [D, TMY], F32),
                 ("x2f_h", [D, TMY], F32), ("r2_h", [D, TMY], F32),
                 ("ds_h", [3, 22528], BF), ("agd0_in", [SLICE_EL], BF),
                 ("agd2_in", [SLICE_EL], BF), ("acts_d", [80, 2048], BF),
                 ("s1_h", [3, 22528], BF), ("psc_d", [128, 16], F32),
                 ("d88f", [3, 88, 768], F32), ("d88b", [3, 88, 1216], BF)]
    dbg_out = {}
    if dbg:
        for nm, shp, dt_ in dbg_specs:
            dbg_out[nm] = nc.dram_tensor("d_" + nm, shp, dt_,
                                         kind="ExternalOutput").ap()

    xbf_h = nc.dram_tensor("xbf_h", [D, TS], BF).ap()
    q_hbm = nc.dram_tensor("q_hbm", [D, TMY], BF).ap()
    k_hbm = nc.dram_tensor("k_hbm", [D, TS], BF).ap()
    vt_hbm = nc.dram_tensor("vt_hbm", [TS, D], BF).ap()
    att_h = nc.dram_tensor("att_h", [D, TMY], F32).ap()
    sums_h = nc.dram_tensor("sums_h", [H, TMY], F32).ap()
    r_h = nc.dram_tensor("r_h", [D, TMY], F32).ap()
    x2f_h = nc.dram_tensor("x2f_h", [D, TMY], F32).ap()
    r2_h = nc.dram_tensor("r2_h", [D, TMY], F32).ap()
    rcp_h = nc.dram_tensor("rcp_h", [H, TMY], F32).ap()
    s1_h = nc.dram_tensor("s1_h", [3, 22528], BF).ap()
    ds_h = nc.dram_tensor("ds_h", [3, 22528], BF).ap()
    agw_in = [nc.dram_tensor(f"agw{w}_in", [256, D], BF).ap()
              for w in range(4)]
    agw_out = [nc.dram_tensor(f"agw{w}_out", [8 * 256, D], BF,
                              addr_space="Shared").ap() for w in range(4)]
    agp_in = nc.dram_tensor("agp_in", [1, 8], F32).ap()
    agp_out = nc.dram_tensor("agp_out", [8, 8], F32, addr_space="Shared").ap()
    agd_in = [nc.dram_tensor(f"agd{w}_in", [SLICE_EL], BF).ap()
              for w in range(3)]
    agd_out = [nc.dram_tensor(f"agd{w}_out", [8, SLICE_EL], BF,
                              addr_space="Shared").ap() for w in range(3)]

    from contextlib import ExitStack
    ctx = ExitStack()

    def sb(name, shape, dtype):
        return ctx.enter_context(nc.sbuf_tensor(name, shape, dtype))

    U = sb("U", [128, 45056], BF)
    x2b = sb("x2b", [128, 16384], BF)
    wcolA = sb("wcolA", [128, 4096], BF)
    wcolB = sb("wcolB", [128, 4096], BF)
    qh2 = sb("qh2", [128, 1024], BF)
    pt3 = sb("pt3", [128, 1536], BF)
    sA = sb("sA", [128, 1024], F32)
    sB = sb("sB", [128, 1024], F32)
    sC = sb("sC", [128, 1024], F32)
    rx2 = sb("rx2", [128, 2048], F32)
    acts = sb("acts", [80, 2048], BF)
    acts_o = sb("acts_o", [16, 1024], BF)
    ballt_sb = sb("ballt_sb", [128, 1280], BF)
    bot_sb = sb("bot_sb", [128, 256], BF)
    at4_sb = sb("at4_sb", [80, 4096], BF)
    bcols_sb = sb("bcols_sb", [128, 48], F32)
    onesrb = sb("onesrb", [1, 128], BF)
    eyef = sb("eyef", [128, 128], F32)
    onesf = sb("onesf", [128, 1], F32)
    onesb = sb("onesb", [128, 1], BF)
    onesrow = sb("onesrow", [1, 128], F32)
    lnrow = sb("lnrow", [1, 2048], F32)
    dsc88f = sb("dsc88f", [88, 768], F32)
    psc = sb("psc", [128, 16], F32)
    dum = sb("dum", [1, 16], F32)
    ds1f = sb("ds1f_t", [128, 32], F32)
    dsrec = sb("dsrec_t", [128, 32], F32)
    dstf = sb("dstf_t", [128, 32], F32)
    dwp = sb("dwp_t", [128, 2048], F32)
    dwt = x2b[:, 0:4096]
    dacc = x2b[:, 4096:6144]
    dplane = x2b[:, 6144:8192]
    dstg = x2b[:, 8192:10240]
    ds1 = x2b[:, 14336:14912]
    dst2 = x2b[:, 14912:14976]
    dsc88 = x2b[0:88, 15168:16384]
    kh2 = U[:, 0:4096]
    vt2 = U[:, 4096:8192]
    sums16 = rx2[0:16, 0:1024]
    recip16 = rx2[0:16, 1024:2048]
    bvrep_sb = rx2

    PS = [ctx.enter_context(nc.psum_tensor(f"PS{i}", [128, 512], F32))
          for i in range(8)]

    sems = {}
    for name in ("q_in", "q_g", "q_sc", "q_out", "cc", "pe", "ve", "se",
                 "bt", "bv", "bs", "by", "bg"):
        sems[name] = Ctr(ctx.enter_context(nc.semaphore(name)))
    q_in, q_g, q_sc = sems["q_in"], sems["q_g"], sems["q_sc"]
    q_out, cc = sems["q_out"], sems["cc"]
    pe, ve, se = sems["pe"], sems["ve"], sems["se"]

    P = Plan()

    def emit_block():
        with nc.Block() as block:
            @block.tensor
            def _(e):
                for fn in P.ops["tensor"]:
                    fn(e)

            @block.vector
            def _(e):
                for fn in P.ops["vector"]:
                    fn(e)

            @block.scalar
            def _(e):
                for fn in P.ops["scalar"]:
                    fn(e)

            @block.sync
            def _(e):
                for fn in P.ops["sync"]:
                    fn(e)

            @block.gpsimd
            def _(e):
                for fn in P.ops["gpsimd"]:
                    fn(e)
        ctx.close()

    bsem = {"tensor": sems["bt"], "vector": sems["bv"], "scalar": sems["bs"],
            "sync": sems["by"], "gpsimd": sems["bg"]}
    bar_n = [0]

    def barrier():
        bar_n[0] += 1
        n = bar_n[0]
        for e in Plan.ENGINES:
            P.ops[e].append(lambda eng, s=bsem[e]: eng.nop().then_inc(s.h, 1))
        for e in Plan.ENGINES:
            for o in Plan.ENGINES:
                if o != e:
                    P.ops[e].append(
                        lambda eng, s=bsem[o], v=n: eng.wait_ge(s.h, v))

    def W(put, sem, val):
        if val and val > 0:
            put(lambda e, s=sem, v=val: e.wait_ge(s.h, v))

    def dma(put, q, dst, src):
        n = q.inc(16)
        put(lambda e, dd=dst, ss=src, qq=q: e.dma_start(out=dd, in_=ss)
            .then_inc(qq.h, 16))
        return n

    def t_mm(outap, lhsT, rhs, start, stop, inc=False):
        if inc:
            n = pe.inc(1)
            P.t(lambda e, o=outap, l=lhsT, r=rhs, st=start, sp=stop:
                e.matmul(o, l, r, start=st, stop=sp).then_inc(pe.h, 1))
            return n
        P.t(lambda e, o=outap, l=lhsT, r=rhs, st=start, sp=stop:
            e.matmul(o, l, r, start=st, stop=sp))
        return None

    def v_op(f, inc=False):
        if inc:
            n = ve.inc(1)
            P.v(lambda e, ff=f: ff(e).then_inc(ve.h, 1))
            return n
        P.v(lambda e, ff=f: ff(e))
        return None

    def v_seq(f):
        W(P.v, ve, ve.n)
        return v_op(f, inc=True)

    def fence(put, q):
        return dma(put, q, dum[0:1, 0:1], onesf_d[0:1, 0:1])

    def s_op(f, inc=False):
        if inc:
            n = se.inc(1)
            P.s(lambda e, ff=f: ff(e).then_inc(se.h, 1))
            return n
        P.s(lambda e, ff=f: ff(e))
        return None

    def xv(i):
        return U[:, 2048 * i:2048 * (i + 1)]

    def finish_early():
        nv = v_op(lambda e: e.memset(sA[:], 0.0), inc=True)
        W(P.y, ve, nv)
        for k in range(16):
            for f in range(2):
                dma(P.y, q_out, out[128 * k:128 * (k + 1),
                                    512 * f:512 * (f + 1)],
                    sA[:, 512 * f:512 * f + 512])
        dma(P.y, q_out, r_h[0:1, 0:1], dum[0:1, 0:1])
        P.y(lambda e: e.wait_ge(q_out.h, 16 * 33))
        emit_block()

    for dst, src in ((eyef[:], eyef_d[:]), (onesf[:], onesf_d[:]),
                     (onesrow[:], onesrow_d[:]), (bcols_sb[:], bcols[:])):
        dma(P.y, q_in, dst, src)
    n_const = q_in.n

    dma(P.g, q_g, onesb[:], onesf_d[:])
    dma(P.g, q_g, onesrb[:], onesrow_d[:])
    n_bvrep = dma(P.y, q_in, bvrep_sb[:, :], bvrep[:])
    g_x = dma(P.g, q_g, xbf_h[:], xt[:])
    dma(P.g, q_g,
        ballt_sb[:].rearrange("p (i c) -> p i c", c=80),
        ballt[:].rearrange("(i p) c -> p i c", p=128))
    dma(P.g, q_g,
        bot_sb[:].rearrange("p (i c) -> p i c", c=16),
        bot[:].rearrange("(i p) c -> p i c", p=128))
    g_small = dma(P.g, q_g, at4_sb[:], at4[:])
    for w in range(4):
        g_w = dma(P.g, q_g, agw_in[w], wts[w])

    def collective(ins_, outs_):
        n = cc.inc(1)
        if n > 1:
            P.g(lambda e, v=n - 1: e.wait_ge(cc.h, v))
        P.g(lambda e, i=ins_, o=outs_: e.collective_compute(
            "AllGather", ALU.bypass, replica_groups=[list(range(8))],
            ins=[i], outs=[o]).then_inc(cc.h, 1))
        return n

    g_w = fence(P.g, q_g)
    W(P.g, q_g, g_w)
    agw_n = []
    for w in range(4):
        agw_n.append(collective(agw_in[w].opt(), agw_out[w].opt()))

    DT = []
    for r in range(6):
        p = 128 if r < 5 else 64
        DT.append((0, w1s, 128 * r, p, 0, 2048, 32, 32 * r))
    for r in range(6):
        p = 128 if r < 5 else 64
        DT.append((1, w2s, 128 * r, p, 0, 2048, 32, 32 * r + 192))
    for r in range(2):
        for (c0, ncols) in ((0, 2048), (2048, 2048), (4096, 1536)):
            DT.append((2, w3s, 128 * r, 128, c0, ncols, 88,
                       384 + 88 * r + c0 // 64))

    v_op(lambda e: e.memset(psc[:], 0.0))
    v_op(lambda e: e.memset(psc[:, 15:16], -C0))
    vred = []
    ld_n = []

    def _emit_passA_load(di):
        w, wd, r0, p, c0, ncols, segw, s1c0 = DT[di]
        buf = di % 2
        if di >= 2:
            W(P.y, ve, vred[di - 2])
        ld_n.append(dma(P.y, q_in, dwt[0:p, 2048 * buf:2048 * buf + ncols],
                        wd[r0:r0 + p, c0:c0 + ncols]))

    _emit_passA_load(0)
    _emit_passA_load(1)
    for di, (w, wd, r0, p, c0, ncols, segw, s1c0) in enumerate(DT):
        buf = di % 2
        nseg = ncols // 64
        W(P.v, q_in, ld_n[min(di + 1, len(ld_n) - 1)]
          if di + 1 < len(DT) else ld_n[di])
        s1t = ds1[0:p, s1c0:s1c0 + nseg]
        for g in range(nseg):
            nlast = v_op(lambda e, p_=p, b=buf, g_=g:
                         e.reduce_max(ds1f[0:p_, g_:g_ + 1],
                                      dwt[0:p_, 2048 * b + 64 * g_:
                                          2048 * b + 64 * g_ + 64],
                                      axis=AX.X, apply_absolute_value=True),
                         inc=(g == nseg - 1))
        W(P.v, ve, nlast)
        v_seq(lambda e, s=s1t, p_=p, ns=nseg:
              e.tensor_copy(s, ds1f[0:p_, 0:ns]))
        v_seq(lambda e, s=s1t, p_=p: e.reduce_sum(sA[0:p_, 0:1], s,
                                                  axis=AX.X))
        nv = v_seq(lambda e, p_=p, w_=w:
                   e.tensor_tensor(psc[0:p_, w_:w_ + 1], psc[0:p_, w_:w_ + 1],
                                   sA[0:p_, 0:1], ALU.add))
        vred.append(nv)
        if di + 2 < len(DT):
            _emit_passA_load(di + 2)
        W(P.s, ve, nv)
        dma(P.s, q_sc, s1_h[w].rearrange("(r c) -> r c", c=segw)
            [r0:r0 + p, c0 // 64:c0 // 64 + nseg], s1t)
    g_s1done = ("q_sc", fence(P.s, q_sc))

    W(P.t, ve, vred[-1])
    W(P.t, q_in, n_const)
    for w in range(3):
        t_mm(PS[0][0:1, 0:1], onesf[:], psc[:, w:w + 1], True, True, inc=True)
        n_pe = pe.n
        W(P.v, pe, n_pe)
        nvc = v_op(lambda e, w_=w: e.tensor_copy(psc[0:1, 4 + w_:5 + w_],
                                                 PS[0][0:1, 0:1]), inc=True)
        W(P.t, ve, nvc)
    n_part = ve.n
    W(P.g, ve, n_part)
    g_agp = dma(P.g, q_g, agp_in[:], psc[0:1, 4:12])
    g_agp = fence(P.g, q_g)
    W(P.g, q_g, g_agp)
    cc_agp = collective(agp_in.opt(), agp_out.opt())
    W(P.y, cc, cc_agp)
    n_agp = dma(P.y, q_in, sA[0:8, 0:8], agp_out[:])
    n_agp = fence(P.y, q_in)
    W(P.t, q_in, n_agp)
    t_mm(PS[0][0:1, 0:8], onesf[0:8, 0:1], sA[0:8, 0:8], True, True,
         inc=True)
    W(P.v, pe, pe.n)
    v_seq(lambda e: e.tensor_copy(psc[0:1, 8:11], PS[0][0:1, 0:3]))
    v_seq(lambda e: e.tensor_single_scalar(psc[0:1, 8:11], psc[0:1, 8:11],
                                           1.0 / NBLK, ALU.mult))
    v_seq(lambda e: e.tensor_copy(dsc88[0:1, 1024:1027], psc[0:1, 8:11]))
    nv_mean = v_seq(lambda e: e.tensor_copy(psc[0:1, 12:15],
                                            dsc88[0:1, 1024:1027]))
    W(P.t, ve, nv_mean)
    for w in range(3):
        t_mm(PS[1][:, 0:1], onesrow[:], psc[0:1, 12 + w:13 + w], True, True,
             inc=True)
        W(P.v, pe, pe.n)
        nvc = v_op(lambda e, w_=w: e.tensor_copy(psc[:, 4 + w_:5 + w_],
                                                 PS[1][:, 0:1]), inc=True)
        W(P.t, ve, nvc)
    nv_mrep = ve.n

    W(P.y, q_sc, g_s1done[1])
    prev_ds_store = None
    for w in range(3):
        W(P.y, q_g, prev_ds_store)
        n_s88 = dma(P.y, q_in, dsc88[0:88, 0:256],
                    s1_h[w].rearrange("(r c) -> r c", c=256))
        W(P.v, q_in, n_s88)
        v_seq(lambda e, w_=w: e.tensor_single_scalar(
            dsc88[0:88, 256:512], dsc88[0:88, 0:256],
            psc[0:88, 4 + w_:5 + w_], ALU.subtract))
        v_seq(lambda e: e.reduce_max(dsc88f[0:88, 0:1],
                                    dsc88[0:88, 256:512], axis=AX.X,
                                    apply_absolute_value=True))
        v_seq(lambda e: e.reciprocal(dsc88f[0:88, 1:2], dsc88f[0:88, 0:1]))
        v_seq(lambda e: e.tensor_single_scalar(dsc88[0:88, 514:515],
                                              dsc88f[0:88, 1:2], 128.0,
                                              ALU.mult))
        v_seq(lambda e: e.tensor_copy(dsc88f[0:88, 2:3], dsc88[0:88, 514:515]))
        v_seq(lambda e: e.reciprocal(dsc88f[0:88, 3:4], dsc88f[0:88, 2:3]))
        v_seq(lambda e: e.tensor_single_scalar(
            dsc88[0:88, 768:1024], dsc88[0:88, 256:512], dsc88f[0:88, 2:3],
            ALU.mult))
        v_seq(lambda e: e.tensor_scalar(dsc88f[0:88, 256:512],
                                       dsc88[0:88, 768:1024], RND_C, RND_C,
                                       ALU.add, ALU.subtract))
        v_seq(lambda e: e.tensor_scalar(dsc88f[0:88, 512:768],
                                       dsc88f[0:88, 256:512], -128.0, 127.0,
                                       ALU.max, ALU.min))
        v_seq(lambda e: e.tensor_single_scalar(dsc88[0:88, 768:1024],
                                              dsc88f[0:88, 512:768],
                                              dsc88f[0:88, 3:4], ALU.mult))
        nv = v_seq(lambda e, w_=w: e.tensor_single_scalar(
            dsc88[0:88, 0:256], dsc88[0:88, 768:1024],
            psc[0:88, 4 + w_:5 + w_], ALU.add))
        W(P.g, ve, nv)
        dma(P.g, q_g, ds_h[w].rearrange("(r c) -> r c", c=256),
            dsc88[0:88, 0:256])
        prev_ds_store = fence(P.g, q_g)
        if dbg:
            dma(P.g, q_g, dbg_out["d88f"][w], dsc88f[0:88, :])
            prev_ds_store = dma(P.g, q_g, dbg_out["d88b"][w],
                                dsc88[0:88, :])
    g_dsdone = q_g.n

    barrier()
    if ph <= 0:
        finish_early()
        return nc

    W(P.y, q_g, g_w)
    n_xb = dma(P.y, q_in, U[:, 0:32768].rearrange("p (i c) -> p i c", c=2048),
               xbf_h[:].rearrange("(i p) c -> p i c", p=128))
    n_xb = fence(P.y, q_in)

    passB = {"wp_ready": [None] * len(DT), "tr_done": [None] * len(DT),
             "stg_stored": [None] * len(DT)}
    tr_ev_ring = []
    last_di_of_w = {w: max(di for di in range(len(DT)) if DT[di][0] == w)
                    for w in range(3)}
    cc_agd = {}
    cc_tail = [None]

    def emit_passB_tile(di):
        w, wd, r0, p, c0, ncols, segw, s1c0 = DT[di]
        nseg = ncols // 64
        buf = di % 2
        if di >= 2:
            W(P.y, ve, passB["wp_ready"][di - 2])
        n_ld = dma(P.y, q_in, dwt[0:p, 2048 * buf:2048 * buf + ncols],
                   wd[r0:r0 + p, c0:c0 + ncols])
        n_ds = dma(P.y, q_in, dst2[0:p, 32 * buf:32 * buf + nseg],
                   ds_h[w].rearrange("(r c) -> r c", c=segw)
                   [r0:r0 + p, c0 // 64:c0 // 64 + nseg])
        fence(P.y, q_in)
        n_ld = n_ds + 16
        s1t = ds1[0:p, s1c0:s1c0 + nseg]
        v_seq(lambda e, s=s1t, p_=p, ns=nseg:
              e.tensor_copy(ds1f[0:p_, 0:ns], s))
        v_seq(lambda e, p_=p, ns=nseg: e.reciprocal(dsrec[0:p_, 0:ns],
                                                    ds1f[0:p_, 0:ns]))
        W(P.v, q_in, n_ld)
        W(P.v, ve, ve.n)
        for g in range(nseg):
            nlast = v_op(lambda e, p_=p, b=buf, g_=g: e.tensor_single_scalar(
                dplane[0:p_, 64 * g_:64 * g_ + 64],
                dwt[0:p_, 2048 * b + 64 * g_:2048 * b + 64 * g_ + 64],
                dsrec[0:p_, g_:g_ + 1], ALU.mult), inc=(g == nseg - 1))
        W(P.v, ve, nlast)
        v_op(lambda e, p_=p, nc_=ncols: e.memset(dacc[0:p_, 0:nc_], -1.0))
        for j in range(15):
            v_op(lambda e, p_=p, nc_=ncols, b=buf, tj=NF4_T[j], dj=NF4_D[j]:
                 e.tensor_scalar(dwt[0:p_, 2048 * b:2048 * b + nc_],
                                 dplane[0:p_, 0:nc_], tj, dj,
                                 ALU.is_ge, ALU.mult))
            v_op(lambda e, p_=p, nc_=ncols, b=buf: e.tensor_tensor(
                dacc[0:p_, 0:nc_], dacc[0:p_, 0:nc_],
                dwt[0:p_, 2048 * b:2048 * b + nc_], ALU.add))
        if di >= 1:
            W(P.v, pe, passB["tr_done"][di - 1])
        W(P.v, q_in, n_ds + 16)
        v_seq(lambda e, p_=p, ns=nseg, b=buf: e.tensor_copy(
            dstf[0:p_, 0:ns], dst2[0:p_, 32 * b:32 * b + ns]))
        W(P.v, ve, ve.n)
        nwp = None
        for g in range(nseg):
            nwp = v_op(lambda e, p_=p, g_=g: e.tensor_single_scalar(
                dwp[0:p_, 64 * g_:64 * g_ + 64],
                dacc[0:p_, 64 * g_:64 * g_ + 64],
                dstf[0:p_, g_:g_ + 1], ALU.mult), inc=(g == nseg - 1))
        passB["wp_ready"][di] = nwp

    def emit_transpose_batch(di):
        w, wd, r0, p, c0, ncols, segw, s1c0 = DT[di]
        nch = ncols // 128
        W(P.t, ve, passB["wp_ready"][di])
        prev_store = passB["stg_stored"][di - 1] if di >= 1 else None
        for ci in range(nch):
            bank = 5 + ci % 3
            if len(tr_ev_ring) >= 3:
                W(P.t, ve, tr_ev_ring[-3])
            n_tr = pe.inc(1)
            P.t(lambda e, p_=p, ci_=ci, bk=bank:
                e.transpose(PS[bk][:, 0:p_],
                            dwp[0:p_, 128 * ci_:128 * (ci_ + 1)],
                            eyef[0:p_, 0:p_]).then_inc(pe.h, 1))
            W(P.v, pe, n_tr)
            if ci == 0 and prev_store:
                W(P.v, q_g, prev_store)
            nev = v_op(lambda e, p_=p, ci_=ci, bk=bank: e.tensor_copy(
                dstg[:, p_ * ci_:p_ * (ci_ + 1)],
                PS[bk][:, 0:p_]), inc=True)
            tr_ev_ring.append(nev)
        passB["tr_done"][di] = pe.n
        W(P.g, ve, tr_ev_ring[-1])
        if w < 2:
            dest = agd_in[w].rearrange("(i p f) -> p i f", i=16, p=128)[
                :, :, r0:r0 + p]
            src = dstg[:, 0:16 * p].rearrange("p (i f) -> p i f", f=p)
        else:
            j0 = c0 // 128
            dest = agd_in[2].rearrange("(j p o) -> p j o", j=44, p=128)[
                :, j0:j0 + nch, r0:r0 + p]
            src = dstg[:, 0:nch * p].rearrange("p (j o) -> p j o", o=p)
        passB["stg_stored"][di] = dma(P.g, q_g, dest, src)

    W(P.t, q_in, n_xb)
    W(P.t, q_g, g_small)

    n_acts = None
    for tq in range(4):
        for i in range(16):
            t_mm(PS[4][0:80, :], ballt_sb[:, 80 * i:80 * (i + 1)],
                 xv(i)[:, 512 * tq:512 * (tq + 1)], i == 0, i == 15,
                 inc=(i == 15))
        n_pe = pe.n
        W(P.v, pe, n_pe)
        n_acts = v_op(lambda e, tq_=tq: e.tensor_single_scalar(
            acts[0:80, 512 * tq_:512 * (tq_ + 1)], PS[4][0:80, :], 0.5,
            ALU.mult), inc=True)
        if tq < 3:
            W(P.t, ve, n_acts)

    def load_wcol(put, wsel, j, buf):
        dstap = (wcolA if buf == 0 else wcolB)[:, 0:2048].rearrange(
            "p (i c) -> p i c", c=128)
        srcap = agw_out[wsel].rearrange("(i p) c -> p i c", p=128)[
            :, :, 128 * j:128 * (j + 1)]
        return dma(put, q_in, dstap, srcap)

    W(P.y, cc, agw_n[0])
    unit_no = [0]
    trq = list(range(len(DT)))
    emitted_B = [0]

    def maybe_interleave():
        u = unit_no[0]
        unit_no[0] += 1
        if p1sub < 5:
            return
        want_B = min(len(DT), 2 + u // 8)
        while emitted_B[0] < want_B:
            di = emitted_B[0]
            if di >= 1 and trq and trq[0] == di - 1:
                emit_transpose_batch(trq.pop(0))
            emit_passB_tile(di)
            emitted_B[0] += 1

    psum_user = [None] * 8
    stg_store = {}
    wcol_pe = {}

    def qk_units(wsel, kind, tqs, dram_dst, bcol0, prev_kind):
        for j in range(16):
            buf = j % 2
            if j == 0:
                W(P.y, cc, agw_n[wsel] + 1)
            if j >= 2:
                W(P.y, pe, wcol_pe[(kind, j - 2)])
            elif prev_kind is not None:
                W(P.y, pe, wcol_pe[(prev_kind, 14 + j)])
            n_w = load_wcol(P.y, wsel, j, buf)
            if j == 15:
                fence(P.y, q_in)
            for tq in tqs:
                u = unit_no[0]
                ps = u % 4
                if psum_user[ps] is not None:
                    W(P.t, psum_user[ps][0], psum_user[ps][1])
                W(P.t, q_in, n_w + 16)
                wc = wcolA if buf == 0 else wcolB
                for i in range(16):
                    t_mm(PS[ps][:, :], wc[:, 128 * i:128 * (i + 1)],
                         xv(i)[:, 512 * tq:512 * (tq + 1)], i == 0, False)
                arow = 0 if kind == "q" else 32
                npe = t_mm(PS[ps][:, :],
                           at4_sb[arow:arow + 16, 128 * j:128 * j + 128],
                           acts[arow:arow + 16, 512 * tq:512 * (tq + 1)],
                           False, True, inc=True)
                wcol_pe[(kind, j)] = npe
                W(P.s, pe, npe)
                W(P.s, q_sc, stg_store.get(u - 3))
                stg = pt3[:, 512 * (u % 3):512 * (u % 3) + 512]
                nse = s_op(lambda e, ps_=ps, st=stg, bj=bcol0 + j:
                           e.activation(st, PS[ps_][:, :], AF.Identity,
                                        bias=bcols_sb[:, bj:bj + 1]),
                           inc=True)
                psum_user[ps] = (se, nse)
                W(P.s, se, nse)
                stg_store[u] = dma(P.s, q_sc,
                                   dram_dst[128 * j:128 * (j + 1),
                                            512 * tq:512 * (tq + 1)], stg)
                maybe_interleave()

    if p1sub >= 2:
        qk_units(0, "q", (0, 1), q_hbm, 0, None)
    if p1sub >= 3:
        qk_units(1, "k", (0, 1, 2, 3), k_hbm, 16, "q")
    if p1sub < 4:
        barrier()
        finish_early()
        return nc

    vt_pe_last = [wcol_pe[("k", 15)]]

    for oq in range(4):
        if oq == 0:
            W(P.y, cc, agw_n[2] + 1)
        W(P.y, pe, vt_pe_last[0])
        dma(P.y, q_in,
            wcolA[:].rearrange("p (i c) -> p i c", c=512),
            agw_out[2][0:1024, :].rearrange("(i p) c -> p i c", p=128)[
                :, :, 512 * oq:512 * (oq + 1)])
        nldB = dma(P.y, q_in,
                   wcolB[:].rearrange("p (i c) -> p i c", c=512),
                   agw_out[2][1024:2048, :].rearrange("(i p) c -> p i c",
                                                      p=128)[
                       :, :, 512 * oq:512 * (oq + 1)])
        fence(P.y, q_in)
        for tc in range(16):
            u = unit_no[0]
            ps = u % 4
            if psum_user[ps] is not None:
                W(P.t, psum_user[ps][0], psum_user[ps][1])
            W(P.t, q_in, nldB + 16)
            for i in range(16):
                wc = wcolA if i < 8 else wcolB
                ii = i % 8
                t_mm(PS[ps][:, :], xv(i)[:, 128 * tc:128 * (tc + 1)],
                     wc[:, 512 * ii:512 * (ii + 1)], i == 0, False)
            npe = t_mm(PS[ps][:, :], acts[64:80, 128 * tc:128 * (tc + 1)],
                       at4_sb[64:80, 512 * oq:512 * oq + 512],
                       False, True, inc=True)
            vt_pe_last[0] = npe
            W(P.v, pe, npe)
            W(P.v, q_in, n_bvrep)
            W(P.v, q_sc, stg_store.get(u - 3))
            stg = pt3[:, 512 * (u % 3):512 * (u % 3) + 512]
            nve = v_op(lambda e, ps_=ps, st=stg, o_=oq: e.tensor_tensor(
                st, PS[ps_][:, :],
                bvrep_sb[:, 512 * o_:512 * (o_ + 1)], ALU.add), inc=True)
            psum_user[ps] = (ve, nve)
            W(P.s, ve, nve)
            stg_store[u] = dma(P.s, q_sc,
                               vt_hbm[128 * tc:128 * (tc + 1),
                                      512 * oq:512 * (oq + 1)], stg)
            maybe_interleave()

    if p1sub < 5:
        barrier()
        finish_early()
        return nc
    while emitted_B[0] < len(DT):
        di = emitted_B[0]
        if di >= 1 and trq and trq[0] == di - 1:
            emit_transpose_batch(trq.pop(0))
        emit_passB_tile(di)
        emitted_B[0] += 1
    while trq:
        emit_transpose_batch(trq.pop(0))

    for w in range(3):
        W(P.g, q_g, passB["stg_stored"][last_di_of_w[w]])
    W(P.g, q_g, fence(P.g, q_g))
    for w in range(3):
        cc_agd[w] = collective(agd_in[w].opt(), agd_out[w].opt())
    P.g(lambda e, v=cc_agd[2]: e.wait_ge(cc.h, v))

    f2_in, f2_g, f2_sc = fence(P.y, q_in), fence(P.g, q_g), fence(P.s, q_sc)
    barrier()
    if ph <= 1:
        finish_early()
        return nc

    W(P.y, q_sc, f2_sc)
    unit_pe = {}
    exp_se = {}
    av_done_v = [None]
    sm_done_s = [None]
    att_store = {}
    sum_store = {}
    pt_user = [None, None, None]
    kv_pe = {}
    score_pe = {}
    vt_f = {}

    for u in range(32):
        h, f = u // 2, u % 2
        kbuf = h % 2
        if f == 0:
            if h >= 2:
                W(P.y, pe, kv_pe.get(h - 2))
                W(P.s, pe, kv_pe.get(h - 2))
            dma(P.y, q_in, kh2[:, 2048 * kbuf:2048 * (kbuf + 1)],
                k_hbm[128 * h:128 * (h + 1), :])
            dma(P.s, q_sc, vt2[:, 2048 * kbuf:2048 * (kbuf + 1)]
                .rearrange("p (t c) -> p t c", c=128),
                vt_hbm[:, 128 * h:128 * (h + 1)]
                .rearrange("(t p) c -> p t c", p=128))
            vt_f[h] = fence(P.s, q_sc)
        qbuf = u % 2
        W(P.y, pe, unit_pe.get(u - 2))
        n_q = dma(P.y, q_in, qh2[:, 512 * qbuf:512 * (qbuf + 1)],
                  q_hbm[128 * h:128 * (h + 1), 512 * f:512 * (f + 1)])
        W(P.t, q_in, n_q + 32)
        W(P.t, q_sc, vt_f[h])

        def score(kt):
            gkt = 16 * u + kt
            score_pe[gkt] = t_mm(
                PS[kt % 2][:, :],
                kh2[:, 2048 * kbuf + 128 * kt:2048 * kbuf + 128 * kt + 128],
                qh2[:, 512 * qbuf:512 * (qbuf + 1)], True, True, inc=True)

        for kt in (0, 1):
            W(P.t, se, exp_se.get(16 * u + kt - 2))
            score(kt)
        W(P.t, ve, av_done_v[0])
        W(P.t, se, sm_done_s[0])
        W(P.v, q_g, att_store.get(u - 2))
        W(P.s, q_sc, sum_store.get(u - 2))
        for kt in range(16):
            gkt = 16 * u + kt
            slot = gkt % 3
            W(P.s, pe, score_pe[gkt])
            W(P.s, pe, pt_user[slot])
            nse = s_op(lambda e, sps_=kt % 2, sl=slot: e.activation(
                pt3[:, 512 * sl:512 * (sl + 1)], PS[sps_][:, :], AF.Exp,
                bias=psc[:, 15:16], scale=SCALE), inc=True)
            exp_se[gkt] = nse
            W(P.t, se, nse)
            t_mm(PS[2][:, :],
                 vt2[:, 2048 * kbuf + 128 * kt:2048 * kbuf + 128 * kt + 128],
                 pt3[:, 512 * slot:512 * (slot + 1)], kt == 0, kt == 15)
            npe2 = t_mm(PS[3][0:1, :], onesb[:, :],
                        pt3[:, 512 * slot:512 * (slot + 1)], kt == 0,
                        kt == 15, inc=True)
            pt_user[slot] = npe2
            if kt + 2 <= 15:
                score(kt + 2)
        unit_pe[u] = pe.n
        kv_pe[h] = pe.n
        W(P.v, pe, pe.n)
        nev = v_op(lambda e, u_=u: e.tensor_copy(
            sA[:, 512 * (u_ % 2):512 * (u_ % 2) + 512], PS[2][:, :]),
            inc=True)
        av_done_v[0] = nev
        W(P.g, ve, nev)
        att_store[u] = dma(P.g, q_g,
                           att_h[128 * h:128 * (h + 1), 512 * f:512 * (f + 1)],
                           sA[:, 512 * (u % 2):512 * (u % 2) + 512])
        W(P.s, pe, pe.n)
        nss = s_op(lambda e, u_=u: e.copy(
            lnrow[0:1, 512 * (u_ % 2):512 * (u_ % 2) + 512],
            PS[3][0:1, :]), inc=True)
        sm_done_s[0] = nss
        W(P.s, se, nss)
        sum_store[u] = dma(P.s, q_sc,
                           sums_h[h, 512 * f:512 * (f + 1)].unsqueeze(0),
                           lnrow[0:1, 512 * (u % 2):512 * (u % 2) + 512])

    fence(P.y, q_in)
    f3_in, f3_g, f3_sc = fence(P.y, q_in), fence(P.g, q_g), fence(P.s, q_sc)
    barrier()
    if ph <= 2:
        finish_early()
        return nc

    W(P.y, q_sc, f3_sc)
    W(P.y, q_g, f3_g)
    n_sums = dma(P.y, q_in, sums16, sums_h[:])
    n_sums = fence(P.y, q_in)
    W(P.v, q_in, n_sums)
    nv_rec = v_op(lambda e: e.reciprocal(recip16, sums16), inc=True)
    W(P.g, ve, nv_rec)
    g_rcp = dma(P.g, q_g, rcp_h[:], recip16)
    g_rcp = fence(P.g, q_g)

    attb = U
    n_attb_rep_pe = [0]
    norm_ev = {}
    W(P.t, ve, nv_rec)
    for u in range(32):
        h, f = u // 2, u % 2
        W(P.y, ve, norm_ev.get(u - 2))
        nld = dma(P.y, q_in, sC[:, 512 * (u % 2):512 * (u % 2) + 512],
                  att_h[128 * h:128 * (h + 1), 512 * f:512 * (f + 1)])
        W(P.y, q_g, g_rcp)
        nldr = dma(P.y, q_in, lnrow[0:1, 512 * (u % 2):512 * (u % 2) + 512],
                   rcp_h[h, 512 * f:512 * (f + 1)].unsqueeze(0))
        if u == 31:
            nldr = fence(P.y, q_in)
        nld = nldr
        if u >= 2:
            W(P.t, ve, norm_ev.get(u - 2))
        W(P.t, q_in, nldr)
        npe = t_mm(PS[2 + u % 2][:, :], onesrow[:],
                   lnrow[0:1, 512 * (u % 2):512 * (u % 2) + 512], True, True,
                   inc=True)
        W(P.v, pe, npe)
        W(P.v, q_in, nld)
        nev = v_op(lambda e, u_=u, h_=h, f_=f: e.tensor_tensor(
            attb[:, 1024 * h_ + 512 * f_:1024 * h_ + 512 * f_ + 512],
            sC[:, 512 * (u_ % 2):512 * (u_ % 2) + 512],
            PS[2 + u_ % 2][:, :], ALU.mult), inc=True)
        norm_ev[u] = nev
    n_attb = norm_ev[31]

    W(P.t, ve, n_attb)
    n_actso = None
    for tq in range(2):
        for i in range(16):
            t_mm(PS[4][0:16, :], bot_sb[:, 16 * i:16 * (i + 1)],
                 attb[:, 1024 * i + 512 * tq:1024 * i + 512 * tq + 512],
                 i == 0, i == 15, inc=(i == 15))
        W(P.v, pe, pe.n)
        n_actso = v_op(lambda e, tq_=tq: e.tensor_single_scalar(
            acts_o[0:16, 512 * tq_:512 * (tq_ + 1)], PS[4][0:16, :], 0.5,
            ALU.mult), inc=True)
        if tq == 0:
            W(P.t, ve, n_actso)

    o_pe = {}
    o_ev = {}
    o_sq = {}
    o_st = {}
    o_stats_pe = {}
    W(P.t, ve, n_actso)
    for j in range(16):
        buf = j % 2
        if j == 0:
            W(P.y, cc, agw_n[3] + 1)
        if j >= 2:
            W(P.y, pe, o_stats_pe.get(2 * (j - 2) + 1, None))
        n_w = load_wcol(P.y, 3, j, buf)
        W(P.y, ve, o_ev.get(2 * (j - 2) + 1))
        if j < 2:
            W(P.y, ve, nv_rec)
            W(P.y, q_g, g_rcp)
            W(P.y, pe, n_attb_rep_pe[0])
        nldx = dma(P.y, q_in, rx2[:, 1024 * buf:1024 * buf + 1024],
                   xt[128 * j:128 * (j + 1), 0:1024])
        if j == 15:
            fence(P.y, q_in)
        for f in range(2):
            u = 2 * j + f
            ps = u % 2
            W(P.t, ve, o_ev.get(u - 2))
            W(P.t, q_in, n_w + 16)
            wc = wcolA if buf == 0 else wcolB
            for i in range(16):
                t_mm(PS[ps][:, :], wc[:, 128 * i:128 * (i + 1)],
                     attb[:, 1024 * i + 512 * f:1024 * i + 512 * f + 512],
                     i == 0, False)
            npe = t_mm(PS[ps][:, :],
                       at4_sb[0:16, 2048 + 128 * j:2048 + 128 * j + 128],
                       acts_o[0:16, 512 * f:512 * (f + 1)], False, True,
                       inc=True)
            o_pe[u] = npe
            W(P.v, pe, npe)
            W(P.v, q_in, nldx + 16)
            W(P.v, q_g, o_st.get(u - 2))
            W(P.v, se, o_sq.get(u - 2))
            W(P.v, pe, o_stats_pe.get(u - 2))
            nev = v_op(lambda e, ps_=ps, j_=j, f_=f, b=buf:
                       e.scalar_tensor_tensor(
                           sA[:, 512 * ps_:512 * ps_ + 512],
                           PS[ps_][:, :], bcols_sb[:, 32 + j_:33 + j_],
                           rx2[:, 1024 * b + 512 * f_:
                               1024 * b + 512 * f_ + 512],
                           ALU.add, ALU.add), inc=True)
            o_ev[u] = nev
            W(P.g, ve, nev)
            o_st[u] = dma(P.g, q_g,
                          r_h[128 * j:128 * (j + 1), 512 * f:512 * (f + 1)],
                          sA[:, 512 * ps:512 * ps + 512])
            W(P.s, ve, nev)
            nsq = s_op(lambda e, ps_=ps: e.activation(
                sB[:, 512 * ps_:512 * ps_ + 512],
                sA[:, 512 * ps_:512 * ps_ + 512], AF.Square), inc=True)
            o_sq[u] = nsq
            W(P.t, ve, nev)
            W(P.t, se, nsq)
            t_mm(PS[4 + f][0:1, :], onesf[:], sA[:, 512 * ps:512 * ps + 512],
                 j == 0, j == 15)
            npe3 = t_mm(PS[6 + f][0:1, :], onesf[:],
                        sB[:, 512 * ps:512 * ps + 512], j == 0, j == 15,
                        inc=True)
            o_stats_pe[u] = npe3

    def ln_tail():
        W(P.v, pe, pe.n)
        for i, ps in enumerate((4, 5, 6, 7)):
            v_op(lambda e, i_=i, ps_=ps: e.tensor_copy(
                lnrow[0:1, 512 * i_:512 * (i_ + 1)], PS[ps_][0:1, :]))
        v_seq(lambda e: e.tensor_single_scalar(
            lnrow[0:1, 0:1024], lnrow[0:1, 0:1024], 1.0 / D, ALU.mult))
        v_seq(lambda e: e.tensor_single_scalar(
            lnrow[0:1, 1024:2048], lnrow[0:1, 1024:2048], 1.0 / D, ALU.mult))
        v_seq(lambda e: e.tensor_tensor(
            sC[0:1, 0:1024], lnrow[0:1, 0:1024], lnrow[0:1, 0:1024],
            ALU.mult))
        v_seq(lambda e: e.tensor_tensor(
            lnrow[0:1, 1024:2048], lnrow[0:1, 1024:2048], sC[0:1, 0:1024],
            ALU.subtract))
        nv1 = v_seq(lambda e: e.tensor_single_scalar(
            lnrow[0:1, 1024:2048], lnrow[0:1, 1024:2048], EPS, ALU.add))
        W(P.s, ve, nv1)
        ns1 = s_op(lambda e: e.activation(sC[0:1, 0:1024],
                                          lnrow[0:1, 1024:2048], AF.Sqrt),
                   inc=True)
        W(P.v, se, ns1)
        nv2 = v_op(lambda e: e.reciprocal(lnrow[0:1, 1024:2048],
                                          sC[0:1, 0:1024]), inc=True)
        W(P.t, ve, nv2)
        for f in range(2):
            t_mm(PS[4 + f][:, :], onesrow[:],
                 lnrow[0:1, 512 * f:512 * (f + 1)], True, True)
        n = None
        for f in range(2):
            n = t_mm(PS[6 + f][:, :], onesrow[:],
                     lnrow[0:1, 1024 + 512 * f:1024 + 512 * (f + 1)],
                     True, True, inc=(f == 1))
        return n

    n_rep1 = ln_tail()

    f_rst = fence(P.g, q_g)
    nrm_ev = {}
    nrm_st = {}
    for i in range(16):
        buf = i % 2
        W(P.y, ve, nrm_ev.get(i - 2))
        if i < 2:
            W(P.y, ve, o_ev.get(2 * (14 + i) + 1))
        W(P.y, q_g, f_rst)
        nld = dma(P.y, q_in, rx2[:, 1024 * buf:1024 * buf + 1024],
                  r_h[128 * i:128 * (i + 1), :])
        if i == 15:
            nld = fence(P.y, q_in)
        for f in range(2):
            W(P.v, pe, n_rep1)
            W(P.v, q_in, nld + 16 if i < 15 else nld)
            W(P.v, q_g, nrm_st.get((i - 1, f)))
            v_op(lambda e, b=buf, f_=f: e.tensor_tensor(
                sB[:, 512 * f_:512 * f_ + 512],
                rx2[:, 1024 * b + 512 * f_:1024 * b + 512 * f_ + 512],
                PS[4 + f_][:, :], ALU.subtract))
            v_op(lambda e, f_=f: e.tensor_tensor(
                sA[:, 512 * f_:512 * f_ + 512], sB[:, 512 * f_:512 * f_ + 512],
                PS[6 + f_][:, :], ALU.mult))
            nev = v_op(lambda e, i_=i, f_=f: e.tensor_copy(
                x2b[:, 1024 * i_ + 512 * f_:1024 * i_ + 512 * f_ + 512],
                sA[:, 512 * f_:512 * f_ + 512]), inc=True)
            nrm_ev[i] = nev
            W(P.g, ve, nev)
            nrm_st[(i, f)] = dma(P.g, q_g,
                                 x2f_h[128 * i:128 * (i + 1),
                                       512 * f:512 * (f + 1)],
                                 sA[:, 512 * f:512 * f + 512])

    f4_in, f4_g, f4_sc = fence(P.y, q_in), fence(P.g, q_g), fence(P.s, q_sc)
    barrier()
    if ph <= 3:
        finish_early()
        return nc

    W(P.t, cc, cc_agd[2])
    W(P.y, cc, cc_agd[2])

    def load_fcol(put, w, j, dstbuf):
        f0, f1 = 128 * j, 128 * (j + 1)
        r0, r1 = f0 // 704, (f1 - 1) // 704
        n = None
        for r in range(r0, r1 + 1):
            lo = max(f0, 704 * r)
            hi = min(f1, 704 * (r + 1))
            src = agd_out[w][r].rearrange("(i p f) -> p i f", i=16, p=128)[
                :, :, lo - 704 * r:hi - 704 * r]
            dst = dstbuf.rearrange("p (i c) -> p i c", c=128)[
                :, :, lo - f0:hi - f0]
            n = dma(put, q_in, dst, src)
        return n

    hv = U
    ab_user = [None] * 8
    ab_pe = {}
    ab_h = {}
    for j in range(44):
        buf = j % 2
        if j >= 2:
            W(P.y, pe, ab_pe.get(j - 2))
        load_fcol(P.y, 0, j, (wcolA if buf == 0 else wcolB)[:, 0:2048])
        w2c = rx2[:, 0:1024].bitcast(BF) if buf == 0 \
            else rx2[:, 1024:2048].bitcast(BF)
        n_w2 = load_fcol(P.y, 1, j, w2c)
        if j == 43:
            n_w2 = fence(P.y, q_in)
        for f in range(2):
            pa = 4 * (j % 2) + f
            pb = 4 * (j % 2) + 2 + f
            W(P.t, ve, ab_user[pa])
            W(P.t, ve, ab_user[pb])
            W(P.t, q_in, n_w2 + 16 if j < 43 else n_w2)
            wc1 = (wcolA if buf == 0 else wcolB)[:]
            wc2 = rx2[:, 0:1024].bitcast(BF) if buf == 0 \
                else rx2[:, 1024:2048].bitcast(BF)
            for i in range(16):
                t_mm(PS[pa][:, :], wc1[:, 128 * i:128 * (i + 1)],
                     x2b[:, 1024 * i + 512 * f:1024 * i + 512 * f + 512],
                     i == 0, i == 15)
            for i in range(16):
                t_mm(PS[pb][:, :], wc2[:, 128 * i:128 * (i + 1)],
                     x2b[:, 1024 * i + 512 * f:1024 * i + 512 * f + 512],
                     i == 0, i == 15, inc=(i == 15))
            npe = pe.n
            ab_pe[j] = npe
            W(P.s, pe, npe)
            W(P.s, ve, ab_h.get((j - 1, f)))
            nsi = s_op(lambda e, pa_=pa, f_=f: e.activation(
                sA[:, 512 * f_:512 * f_ + 512], PS[pa_][:, :], AF.Silu),
                inc=True)
            W(P.v, se, nsi)
            nev = v_op(lambda e, pb_=pb, j_=j, f_=f: e.tensor_tensor(
                hv[:, 1024 * j_ + 512 * f_:1024 * j_ + 512 * f_ + 512],
                sA[:, 512 * f_:512 * f_ + 512], PS[pb_][:, :], ALU.mult),
                inc=True)
            ab_user[pa] = nev
            ab_user[pb] = nev
            ab_h[(j, f)] = nev

    f5_in, f5_g, f5_sc = fence(P.y, q_in), fence(P.g, q_g), fence(P.s, q_sc)
    barrier()
    if ph <= 4:
        finish_early()
        return nc

    W(P.t, cc, cc_agd[2])
    W(P.y, cc, cc_agd[2])
    W(P.y, q_g, f4_g)
    w3_pe = {}
    w3_ev = {}
    w3_sq = {}
    w3_st = {}
    w3_stats_pe = {}
    for k in range(16):
        buf = k % 2
        if k >= 2:
            W(P.y, pe, w3_stats_pe.get(2 * (k - 2) + 1))
        dst = x2b[:, 5632 * buf:5632 * (buf + 1)].rearrange(
            "p (j c) -> p j c", c=128)
        src = agd_out[2][k // 2].rearrange("(j p o) -> p j o", j=44, p=128)[
            :, :, 128 * (k % 2):128 * (k % 2) + 128]
        n_w3 = dma(P.y, q_in, dst, src)
        W(P.y, ve, w3_ev.get(2 * (k - 2) + 1))
        nldx = dma(P.y, q_in, rx2[:, 1024 * buf:1024 * buf + 1024],
                   x2f_h[128 * k:128 * (k + 1), :])
        if k == 15:
            fence(P.y, q_in)
        for f in range(2):
            u = 2 * k + f
            ps = 2 * (k % 2) + f
            W(P.t, ve, w3_ev.get(u - 4))
            W(P.t, q_in, n_w3 + 16)
            wc = x2b[:, 5632 * buf:5632 * (buf + 1)]
            for j in range(44):
                t_mm(PS[ps][:, :], wc[:, 128 * j:128 * (j + 1)],
                     hv[:, 1024 * j + 512 * f:1024 * j + 512 * f + 512],
                     j == 0, j == 43, inc=(j == 43))
            npe = pe.n
            w3_pe[u] = npe
            W(P.v, pe, npe)
            W(P.v, q_in, nldx + 16)
            W(P.v, q_g, w3_st.get(u - 2))
            W(P.v, se, w3_sq.get(u - 2))
            W(P.v, pe, w3_stats_pe.get(u - 2))
            nev = v_op(lambda e, ps_=ps, b=buf, f_=f: e.tensor_tensor(
                sA[:, 512 * f_:512 * f_ + 512], PS[ps_][:, :],
                rx2[:, 1024 * b + 512 * f_:1024 * b + 512 * f_ + 512],
                ALU.add), inc=True)
            w3_ev[u] = nev
            W(P.g, ve, nev)
            w3_st[u] = dma(P.g, q_g,
                           r2_h[128 * k:128 * (k + 1), 512 * f:512 * (f + 1)],
                           sA[:, 512 * f:512 * f + 512])
            W(P.s, ve, nev)
            nsq = s_op(lambda e, f_=f: e.activation(
                sB[:, 512 * f_:512 * f_ + 512],
                sA[:, 512 * f_:512 * f_ + 512], AF.Square), inc=True)
            w3_sq[u] = nsq
            W(P.t, ve, nev)
            W(P.t, se, nsq)
            t_mm(PS[4 + f][0:1, :], onesf[:], sA[:, 512 * f:512 * f + 512],
                 k == 0, k == 15)
            npe3 = t_mm(PS[6 + f][0:1, :], onesf[:],
                        sB[:, 512 * f:512 * f + 512], k == 0, k == 15,
                        inc=True)
            w3_stats_pe[u] = npe3

    n_rep2 = ln_tail()

    f_r2 = fence(P.g, q_g)
    out_ev = {}
    out_st = {}
    for k in range(16):
        buf = k % 2
        W(P.y, ve, out_ev.get(k - 2))
        if k < 2:
            W(P.y, ve, w3_ev.get(2 * (14 + k) + 1))
        W(P.y, q_g, f_r2)
        nld = dma(P.y, q_in, rx2[:, 1024 * buf:1024 * buf + 1024],
                  r2_h[128 * k:128 * (k + 1), :])
        fence(P.y, q_in)
        for f in range(2):
            W(P.v, pe, n_rep2)
            W(P.v, q_in, nld + 16)
            W(P.v, q_out, out_st.get((k - 1, f)))
            v_op(lambda e, b=buf, f_=f: e.tensor_tensor(
                sB[:, 512 * f_:512 * f_ + 512],
                rx2[:, 1024 * b + 512 * f_:1024 * b + 512 * f_ + 512],
                PS[4 + f_][:, :], ALU.subtract))
            nev = v_op(lambda e, f_=f: e.tensor_tensor(
                sA[:, 512 * f_:512 * f_ + 512],
                sB[:, 512 * f_:512 * f_ + 512],
                PS[6 + f_][:, :], ALU.mult), inc=True)
            out_ev[k] = nev
            W(P.y, ve, nev)
            out_st[(k, f)] = dma(P.y, q_out,
                                 out[128 * k:128 * (k + 1),
                                     512 * f:512 * (f + 1)],
                                 sA[:, 512 * f:512 * f + 512])
    dma(P.y, q_out, r_h[0:1, 0:1], dum[0:1, 0:1])
    P.y(lambda e: e.wait_ge(q_out.h, 16 * 33))
    if dbg:
        srcs = {"q_hbm": q_hbm, "k_hbm": k_hbm, "vt_hbm": vt_hbm,
                "att_h": att_h, "sums_h": sums_h, "r_h": r_h,
                "x2f_h": x2f_h, "r2_h": r2_h, "ds_h": ds_h,
                "agd0_in": agd_in[0], "agd2_in": agd_in[2], "s1_h": s1_h}
        W(P.g, q_out, 16 * 32)
        for nm, ap_ in srcs.items():
            dma(P.g, q_g, dbg_out[nm], ap_)
        dma(P.g, q_g, dbg_out["acts_d"], acts[0:80, :])
        nacts = dma(P.g, q_g, dbg_out["psc_d"], psc[:])
        P.g(lambda e, v=nacts: e.wait_ge(q_g.h, v))

    emit_block()
    return nc


def shard_inputs(inputs):
    f32 = np.float32
    x = np.ascontiguousarray(np.asarray(inputs["L0_attn_x"], dtype=f32))
    Wq, Wk = inputs["L0_attn_Wq"], inputs["L0_attn_Wk"]
    Wv, Wo = inputs["L0_attn_Wv"], inputs["L0_attn_Wo"]
    bq, bk = inputs["L0_attn_bq"], inputs["L0_attn_bk"]
    bv, bo = inputs["L0_attn_bv"], inputs["L0_attn_bo"]
    Bq, Bk = inputs["L0_attn_Bq"], inputs["L0_attn_Bk"]
    Bv, Bo = inputs["L0_attn_Bv"], inputs["L0_attn_Bo"]
    Aq, Ak = inputs["L0_attn_Aq"], inputs["L0_attn_Ak"]
    Av, Ao = inputs["L0_attn_Av"], inputs["L0_attn_Ao"]
    w1 = np.ascontiguousarray(np.asarray(inputs["L0_ffn_weight1"], dtype=BF16))
    w2 = np.ascontiguousarray(np.asarray(inputs["L0_ffn_weight2"], dtype=BF16))
    w3 = np.ascontiguousarray(np.asarray(inputs["L0_ffn_weight3"], dtype=BF16))

    WT = np.stack([np.ascontiguousarray(np.asarray(W, f32).T)
                   for W in (Wq, Wk, Wv, Wo)])
    bcols = np.zeros((128, 48), f32)
    for wi, b in enumerate((bq, bk, bo)):
        bcols[:, 16 * wi:16 * (wi + 1)] = np.asarray(b, f32).reshape(16, 128).T
    ballt = np.zeros((D, 80), f32)
    for gi, B in zip((0, 32, 64), (Bq, Bk, Bv)):
        ballt[:, gi:gi + 16] = np.asarray(B, f32).T
    ballt = np.ascontiguousarray(ballt)
    botm = np.ascontiguousarray(np.asarray(Bo, f32).T)
    at4m = np.zeros((80, 4096), f32)
    at4m[0:16, 0:2048] = np.asarray(Aq, f32).T
    at4m[0:16, 2048:4096] = np.asarray(Ao, f32).T
    at4m[32:48, 0:2048] = np.asarray(Ak, f32).T
    at4m[64:80, 0:2048] = np.asarray(Av, f32).T
    at4m = np.ascontiguousarray(at4m)
    eyefm = np.eye(128, dtype=f32)
    onesfm = np.ones((128, 1), f32)
    onesrowm = np.ones((1, 128), f32)
    bvrepm = np.ascontiguousarray(
        np.broadcast_to(np.asarray(bv, f32).reshape(1, D), (128, D)).copy())

    in_maps = []
    for c in range(8):
        b, hh = c // 2, c % 2
        xb = x[b]
        if hh == 1:
            xb = np.concatenate([xb[1024:], xb[:1024]], axis=0)
        m = {
            "xt": np.ascontiguousarray(xb.T),
            "wts": np.ascontiguousarray(WT[:, 256 * c:256 * (c + 1), :]),
            "bcols": bcols, "bvrep": bvrepm,
            "ballt": ballt, "bot": botm, "at4": at4m,
            "w1s": np.ascontiguousarray(w1[704 * c:704 * (c + 1)]),
            "w2s": np.ascontiguousarray(w2[704 * c:704 * (c + 1)]),
            "w3s": np.ascontiguousarray(w3[256 * c:256 * (c + 1)]),
            "eyef_d": eyefm, "onesf_d": onesfm, "onesrow_d": onesrowm,
        }
        in_maps.append(m)
    return in_maps


def kernel(**inputs):
    if "nc" not in _CACHED:
        _CACHED["nc"] = build_nc()
    nc = _CACHED["nc"]
    in_maps = shard_inputs(inputs)
    res = run_bass_kernel_spmd(nc, in_maps, core_ids=list(range(8)))
    full = np.zeros((4, TS, D), np.float32)
    for c in range(8):
        b, hh = c // 2, c % 2
        full[b, 1024 * hh:1024 * (hh + 1), :] = \
            np.asarray(res.results[c]["out"]).T
    return full


# version 19 (final)
# speedup vs baseline: 1.0283x; 1.0104x over previous
import numpy as np
import ml_dtypes

import concourse.bass as bass
import concourse.mybir as mybir
from concourse.bass_utils import run_bass_kernel_spmd

BF16 = ml_dtypes.bfloat16
F32 = mybir.dt.float32
BF = mybir.dt.bfloat16
ALU = mybir.AluOpType
AF = mybir.ActivationFunctionType
AX = mybir.AxisListType

NF4_T = [-0.84375, -0.60546875, -0.45703125, -0.337890625, -0.234375,
         -0.1376953125, -0.04541015625, 0.0400390625, 0.12060546875,
         0.2041015625, 0.29296875, 0.390625, 0.50390625, 0.64453125,
         0.86328125]
NF4_D = [0.3046875, 0.171875, 0.12890625, 0.109375, 0.1005859375,
         0.09326171875, 0.09130859375, 0.07958984375, 0.08154296875,
         0.0849609375, 0.091796875, 0.103515625, 0.12109375, 0.16015625,
         0.27734375]
RND_C = 12582912.0
C0 = 20.0
SCALE = float(128) ** -0.5
D = 2048
DFF = 5632
TMY = 1024
TS = 2048
H = 16
NBLK = DFF * D // 64
EPS = 1e-5
SLICE_EL = 2048 * 704

_CACHED = {}


class Ctr:
    __slots__ = ("h", "n")

    def __init__(self, h):
        self.h = h
        self.n = 0

    def inc(self, k=1):
        self.n += k
        return self.n


class Plan:
    ENGINES = ("tensor", "vector", "scalar", "sync", "gpsimd")

    def __init__(self):
        self.ops = {e: [] for e in self.ENGINES}

    def t(self, f):
        self.ops["tensor"].append(f)

    def v(self, f):
        self.ops["vector"].append(f)

    def s(self, f):
        self.ops["scalar"].append(f)

    def y(self, f):
        self.ops["sync"].append(f)

    def g(self, f):
        self.ops["gpsimd"].append(f)


def build_nc(ph=99, p1sub=99, dbg=False):
    nc = bass.Bass("TRN2", target_bir_lowering=False,
                   detect_race_conditions=False)

    xt = nc.dram_tensor("xt", [D, TS], F32, kind="ExternalInput").ap()
    wts = nc.dram_tensor("wts", [4, 256, D], F32, kind="ExternalInput").ap()
    bcols = nc.dram_tensor("bcols", [128, 48], F32, kind="ExternalInput").ap()
    bvrep = nc.dram_tensor("bvrep", [128, D], F32, kind="ExternalInput").ap()
    ballt = nc.dram_tensor("ballt", [D, 80], F32, kind="ExternalInput").ap()
    bot = nc.dram_tensor("bot", [D, 16], F32, kind="ExternalInput").ap()
    at4 = nc.dram_tensor("at4", [80, 4096], F32, kind="ExternalInput").ap()
    w1s = nc.dram_tensor("w1s", [704, D], BF, kind="ExternalInput").ap()
    w2s = nc.dram_tensor("w2s", [704, D], BF, kind="ExternalInput").ap()
    w3s = nc.dram_tensor("w3s", [256, DFF], BF, kind="ExternalInput").ap()
    eyef_d = nc.dram_tensor("eyef_d", [128, 128], F32, kind="ExternalInput").ap()
    onesf_d = nc.dram_tensor("onesf_d", [128, 1], F32, kind="ExternalInput").ap()
    onesrow_d = nc.dram_tensor("onesrow_d", [1, 128], F32,
                               kind="ExternalInput").ap()
    out = nc.dram_tensor("out", [D, TMY], F32, kind="ExternalOutput").ap()
    dbg_specs = [("q_hbm", [D, TMY], BF), ("k_hbm", [D, TS], BF),
                 ("vt_hbm", [TS, D], BF), ("att_h", [D, TMY], F32),
                 ("sums_h", [H, TMY], F32), ("r_h", [D, TMY], F32),
                 ("x2f_h", [D, TMY], F32), ("r2_h", [D, TMY], F32),
                 ("ds_h", [3, 22528], BF), ("agd0_in", [SLICE_EL], BF),
                 ("agd2_in", [SLICE_EL], BF), ("acts_d", [80, 2048], BF),
                 ("s1_h", [3, 22528], BF), ("psc_d", [128, 16], F32),
                 ("d88f", [3, 88, 768], F32), ("d88b", [3, 88, 1216], BF)]
    dbg_out = {}
    if dbg:
        for nm, shp, dt_ in dbg_specs:
            dbg_out[nm] = nc.dram_tensor("d_" + nm, shp, dt_,
                                         kind="ExternalOutput").ap()

    xbf_h = nc.dram_tensor("xbf_h", [D, TS], BF).ap()
    q_hbm = nc.dram_tensor("q_hbm", [D, TMY], BF).ap()
    k_hbm = nc.dram_tensor("k_hbm", [D, TS], BF).ap()
    vt_hbm = nc.dram_tensor("vt_hbm", [TS, D], BF).ap()
    att_h = nc.dram_tensor("att_h", [D, TMY], F32).ap()
    sums_h = nc.dram_tensor("sums_h", [H, TMY], F32).ap()
    r_h = nc.dram_tensor("r_h", [D, TMY], F32).ap()
    x2f_h = nc.dram_tensor("x2f_h", [D, TMY], F32).ap()
    r2_h = nc.dram_tensor("r2_h", [D, TMY], F32).ap()
    rcp_h = nc.dram_tensor("rcp_h", [H, TMY], F32).ap()
    s1_h = nc.dram_tensor("s1_h", [3, 22528], BF).ap()
    ds_h = nc.dram_tensor("ds_h", [3, 22528], BF).ap()
    agw_in = [nc.dram_tensor(f"agw{w}_in", [256, D], BF).ap()
              for w in range(4)]
    agw_out = [nc.dram_tensor(f"agw{w}_out", [8 * 256, D], BF,
                              addr_space="Shared").ap() for w in range(4)]
    agp_in = nc.dram_tensor("agp_in", [1, 8], F32).ap()
    agp_out = nc.dram_tensor("agp_out", [8, 8], F32, addr_space="Shared").ap()
    agd_in = [nc.dram_tensor(f"agd{w}_in", [SLICE_EL], BF).ap()
              for w in range(3)]
    agd_out = [nc.dram_tensor(f"agd{w}_out", [8, SLICE_EL], BF,
                              addr_space="Shared").ap() for w in range(3)]

    from contextlib import ExitStack
    ctx = ExitStack()

    def sb(name, shape, dtype):
        return ctx.enter_context(nc.sbuf_tensor(name, shape, dtype))

    U = sb("U", [128, 45056], BF)
    x2b = sb("x2b", [128, 16384], BF)
    wcolA = sb("wcolA", [128, 4096], BF)
    wcolB = sb("wcolB", [128, 4096], BF)
    qh2 = sb("qh2", [128, 1024], BF)
    pt3 = sb("pt3", [128, 1536], BF)
    sA = sb("sA", [128, 1024], F32)
    sB = sb("sB", [128, 1024], F32)
    sC = sb("sC", [128, 1024], F32)
    rx2 = sb("rx2", [128, 2048], F32)
    acts = sb("acts", [80, 2048], BF)
    acts_o = sb("acts_o", [16, 1024], BF)
    ballt_sb = sb("ballt_sb", [128, 1280], BF)
    bot_sb = sb("bot_sb", [128, 256], BF)
    at4_sb = sb("at4_sb", [80, 4096], BF)
    bcols_sb = sb("bcols_sb", [128, 48], F32)
    onesrb = sb("onesrb", [1, 128], BF)
    eyef = sb("eyef", [128, 128], F32)
    onesf = sb("onesf", [128, 1], F32)
    onesb = sb("onesb", [128, 1], BF)
    onesrow = sb("onesrow", [1, 128], F32)
    lnrow = sb("lnrow", [1, 2048], F32)
    dsc88f = sb("dsc88f", [88, 768], F32)
    psc = sb("psc", [128, 16], F32)
    dum = sb("dum", [1, 16], F32)
    ds1f = sb("ds1f_t", [128, 32], F32)
    dsrec = sb("dsrec_t", [128, 32], F32)
    dstf = sb("dstf_t", [128, 32], F32)
    dwp = sb("dwp_t", [128, 2048], F32)
    dwt = x2b[:, 0:4096]
    dacc = x2b[:, 4096:6144]
    dplane = x2b[:, 6144:8192]
    dstg = x2b[:, 8192:10240]
    ds1 = x2b[:, 14336:14912]
    dst2 = x2b[:, 14912:14976]
    dsc88 = x2b[0:88, 15168:16384]
    kh2 = U[:, 0:4096]
    vt2 = U[:, 4096:8192]
    sums16 = rx2[0:16, 0:1024]
    recip16 = rx2[0:16, 1024:2048]
    bvrep_sb = rx2

    PS = [ctx.enter_context(nc.psum_tensor(f"PS{i}", [128, 512], F32))
          for i in range(8)]

    sems = {}
    for name in ("q_in", "q_g", "q_sc", "q_out", "cc", "pe", "ve", "se",
                 "bt", "bv", "bs", "by", "bg"):
        sems[name] = Ctr(ctx.enter_context(nc.semaphore(name)))
    q_in, q_g, q_sc = sems["q_in"], sems["q_g"], sems["q_sc"]
    q_out, cc = sems["q_out"], sems["cc"]
    pe, ve, se = sems["pe"], sems["ve"], sems["se"]

    P = Plan()

    def emit_block():
        with nc.Block() as block:
            @block.tensor
            def _(e):
                for fn in P.ops["tensor"]:
                    fn(e)

            @block.vector
            def _(e):
                for fn in P.ops["vector"]:
                    fn(e)

            @block.scalar
            def _(e):
                for fn in P.ops["scalar"]:
                    fn(e)

            @block.sync
            def _(e):
                for fn in P.ops["sync"]:
                    fn(e)

            @block.gpsimd
            def _(e):
                for fn in P.ops["gpsimd"]:
                    fn(e)
        ctx.close()

    bsem = {"tensor": sems["bt"], "vector": sems["bv"], "scalar": sems["bs"],
            "sync": sems["by"], "gpsimd": sems["bg"]}
    bar_n = [0]

    def barrier():
        bar_n[0] += 1
        n = bar_n[0]
        for e in Plan.ENGINES:
            P.ops[e].append(lambda eng, s=bsem[e]: eng.nop().then_inc(s.h, 1))
        for e in Plan.ENGINES:
            for o in Plan.ENGINES:
                if o != e:
                    P.ops[e].append(
                        lambda eng, s=bsem[o], v=n: eng.wait_ge(s.h, v))

    def W(put, sem, val):
        if val and val > 0:
            put(lambda e, s=sem, v=val: e.wait_ge(s.h, v))

    def dma(put, q, dst, src):
        n = q.inc(16)
        put(lambda e, dd=dst, ss=src, qq=q: e.dma_start(out=dd, in_=ss)
            .then_inc(qq.h, 16))
        return n

    def t_mm(outap, lhsT, rhs, start, stop, inc=False):
        if inc:
            n = pe.inc(1)
            P.t(lambda e, o=outap, l=lhsT, r=rhs, st=start, sp=stop:
                e.matmul(o, l, r, start=st, stop=sp).then_inc(pe.h, 1))
            return n
        P.t(lambda e, o=outap, l=lhsT, r=rhs, st=start, sp=stop:
            e.matmul(o, l, r, start=st, stop=sp))
        return None

    def v_op(f, inc=False):
        if inc:
            n = ve.inc(1)
            P.v(lambda e, ff=f: ff(e).then_inc(ve.h, 1))
            return n
        P.v(lambda e, ff=f: ff(e))
        return None

    def v_seq(f):
        W(P.v, ve, ve.n)
        return v_op(f, inc=True)

    def fence(put, q):
        return dma(put, q, dum[0:1, 0:1], onesf_d[0:1, 0:1])

    def s_op(f, inc=False):
        if inc:
            n = se.inc(1)
            P.s(lambda e, ff=f: ff(e).then_inc(se.h, 1))
            return n
        P.s(lambda e, ff=f: ff(e))
        return None

    def xv(i):
        return U[:, 2048 * i:2048 * (i + 1)]

    def finish_early():
        nv = v_op(lambda e: e.memset(sA[:], 0.0), inc=True)
        W(P.y, ve, nv)
        for k in range(16):
            for f in range(2):
                dma(P.y, q_out, out[128 * k:128 * (k + 1),
                                    512 * f:512 * (f + 1)],
                    sA[:, 512 * f:512 * f + 512])
        dma(P.y, q_out, r_h[0:1, 0:1], dum[0:1, 0:1])
        P.y(lambda e: e.wait_ge(q_out.h, 16 * 33))
        emit_block()

    for dst, src in ((eyef[:], eyef_d[:]), (onesf[:], onesf_d[:]),
                     (onesrow[:], onesrow_d[:]), (bcols_sb[:], bcols[:])):
        dma(P.y, q_in, dst, src)
    n_const = q_in.n

    dma(P.g, q_g, onesb[:], onesf_d[:])
    dma(P.g, q_g, onesrb[:], onesrow_d[:])
    n_bvrep = dma(P.y, q_in, bvrep_sb[:, :], bvrep[:])
    g_x = dma(P.g, q_g, xbf_h[:], xt[:])
    dma(P.g, q_g,
        ballt_sb[:].rearrange("p (i c) -> p i c", c=80),
        ballt[:].rearrange("(i p) c -> p i c", p=128))
    dma(P.g, q_g,
        bot_sb[:].rearrange("p (i c) -> p i c", c=16),
        bot[:].rearrange("(i p) c -> p i c", p=128))
    g_small = dma(P.g, q_g, at4_sb[:], at4[:])
    for w in range(4):
        g_w = dma(P.g, q_g, agw_in[w], wts[w])

    def collective(ins_, outs_):
        n = cc.inc(1)
        if n > 1:
            P.g(lambda e, v=n - 1: e.wait_ge(cc.h, v))
        P.g(lambda e, i=ins_, o=outs_: e.collective_compute(
            "AllGather", ALU.bypass, replica_groups=[list(range(8))],
            ins=[i], outs=[o]).then_inc(cc.h, 1))
        return n

    g_w = fence(P.g, q_g)
    W(P.g, q_g, g_w)
    agw_n = []
    for w in range(4):
        agw_n.append(collective(agw_in[w].opt(), agw_out[w].opt()))

    DT = []
    for r in range(6):
        p = 128 if r < 5 else 64
        DT.append((0, w1s, 128 * r, p, 0, 2048, 32, 32 * r))
    for r in range(6):
        p = 128 if r < 5 else 64
        DT.append((1, w2s, 128 * r, p, 0, 2048, 32, 32 * r + 192))
    for r in range(2):
        for (c0, ncols) in ((0, 2048), (2048, 2048), (4096, 1536)):
            DT.append((2, w3s, 128 * r, 128, c0, ncols, 88,
                       384 + 88 * r + c0 // 64))

    v_op(lambda e: e.memset(psc[:], 0.0))
    v_op(lambda e: e.memset(psc[:, 15:16], -C0))
    vred = []
    ld_n = []

    def _emit_passA_load(di):
        w, wd, r0, p, c0, ncols, segw, s1c0 = DT[di]
        buf = di % 2
        if di >= 2:
            W(P.y, ve, vred[di - 2])
        ld_n.append(dma(P.y, q_in, dwt[0:p, 2048 * buf:2048 * buf + ncols],
                        wd[r0:r0 + p, c0:c0 + ncols]))

    _emit_passA_load(0)
    _emit_passA_load(1)
    for di, (w, wd, r0, p, c0, ncols, segw, s1c0) in enumerate(DT):
        buf = di % 2
        nseg = ncols // 64
        W(P.v, q_in, ld_n[min(di + 1, len(ld_n) - 1)]
          if di + 1 < len(DT) else ld_n[di])
        s1t = ds1[0:p, s1c0:s1c0 + nseg]
        for g in range(nseg):
            nlast = v_op(lambda e, p_=p, b=buf, g_=g:
                         e.reduce_max(ds1f[0:p_, g_:g_ + 1],
                                      dwt[0:p_, 2048 * b + 64 * g_:
                                          2048 * b + 64 * g_ + 64],
                                      axis=AX.X, apply_absolute_value=True),
                         inc=(g == nseg - 1))
        W(P.v, ve, nlast)
        v_seq(lambda e, s=s1t, p_=p, ns=nseg:
              e.tensor_copy(s, ds1f[0:p_, 0:ns]))
        v_seq(lambda e, s=s1t, p_=p: e.reduce_sum(sA[0:p_, 0:1], s,
                                                  axis=AX.X))
        nv = v_seq(lambda e, p_=p, w_=w:
                   e.tensor_tensor(psc[0:p_, w_:w_ + 1], psc[0:p_, w_:w_ + 1],
                                   sA[0:p_, 0:1], ALU.add))
        vred.append(nv)
        if di + 2 < len(DT):
            _emit_passA_load(di + 2)
        W(P.s, ve, nv)
        dma(P.s, q_sc, s1_h[w].rearrange("(r c) -> r c", c=segw)
            [r0:r0 + p, c0 // 64:c0 // 64 + nseg], s1t)
    g_s1done = ("q_sc", fence(P.s, q_sc))

    W(P.t, ve, vred[-1])
    W(P.t, q_in, n_const)
    for w in range(3):
        t_mm(PS[0][0:1, 0:1], onesf[:], psc[:, w:w + 1], True, True, inc=True)
        n_pe = pe.n
        W(P.v, pe, n_pe)
        nvc = v_op(lambda e, w_=w: e.tensor_copy(psc[0:1, 4 + w_:5 + w_],
                                                 PS[0][0:1, 0:1]), inc=True)
        W(P.t, ve, nvc)
    n_part = ve.n
    W(P.g, ve, n_part)
    g_agp = dma(P.g, q_g, agp_in[:], psc[0:1, 4:12])
    g_agp = fence(P.g, q_g)
    W(P.g, q_g, g_agp)
    cc_agp = collective(agp_in.opt(), agp_out.opt())
    W(P.y, cc, cc_agp)
    n_agp = dma(P.y, q_in, sA[0:8, 0:8], agp_out[:])
    n_agp = fence(P.y, q_in)
    W(P.t, q_in, n_agp)
    t_mm(PS[0][0:1, 0:8], onesf[0:8, 0:1], sA[0:8, 0:8], True, True,
         inc=True)
    W(P.v, pe, pe.n)
    v_seq(lambda e: e.tensor_copy(psc[0:1, 8:11], PS[0][0:1, 0:3]))
    v_seq(lambda e: e.tensor_single_scalar(psc[0:1, 8:11], psc[0:1, 8:11],
                                           1.0 / NBLK, ALU.mult))
    v_seq(lambda e: e.tensor_copy(dsc88[0:1, 1024:1027], psc[0:1, 8:11]))
    nv_mean = v_seq(lambda e: e.tensor_copy(psc[0:1, 12:15],
                                            dsc88[0:1, 1024:1027]))
    W(P.t, ve, nv_mean)
    for w in range(3):
        t_mm(PS[1][:, 0:1], onesrow[:], psc[0:1, 12 + w:13 + w], True, True,
             inc=True)
        W(P.v, pe, pe.n)
        nvc = v_op(lambda e, w_=w: e.tensor_copy(psc[:, 4 + w_:5 + w_],
                                                 PS[1][:, 0:1]), inc=True)
        W(P.t, ve, nvc)
    nv_mrep = ve.n

    W(P.y, q_sc, g_s1done[1])
    prev_ds_store = None
    for w in range(3):
        W(P.y, q_g, prev_ds_store)
        n_s88 = dma(P.y, q_in, dsc88[0:88, 0:256],
                    s1_h[w].rearrange("(r c) -> r c", c=256))
        W(P.v, q_in, n_s88)
        v_seq(lambda e, w_=w: e.tensor_single_scalar(
            dsc88[0:88, 256:512], dsc88[0:88, 0:256],
            psc[0:88, 4 + w_:5 + w_], ALU.subtract))
        v_seq(lambda e: e.reduce_max(dsc88f[0:88, 0:1],
                                    dsc88[0:88, 256:512], axis=AX.X,
                                    apply_absolute_value=True))
        v_seq(lambda e: e.reciprocal(dsc88f[0:88, 1:2], dsc88f[0:88, 0:1]))
        v_seq(lambda e: e.tensor_single_scalar(dsc88[0:88, 514:515],
                                              dsc88f[0:88, 1:2], 128.0,
                                              ALU.mult))
        v_seq(lambda e: e.tensor_copy(dsc88f[0:88, 2:3], dsc88[0:88, 514:515]))
        v_seq(lambda e: e.reciprocal(dsc88f[0:88, 3:4], dsc88f[0:88, 2:3]))
        v_seq(lambda e: e.tensor_single_scalar(
            dsc88[0:88, 768:1024], dsc88[0:88, 256:512], dsc88f[0:88, 2:3],
            ALU.mult))
        v_seq(lambda e: e.tensor_scalar(dsc88f[0:88, 256:512],
                                       dsc88[0:88, 768:1024], RND_C, RND_C,
                                       ALU.add, ALU.subtract))
        v_seq(lambda e: e.tensor_scalar(dsc88f[0:88, 512:768],
                                       dsc88f[0:88, 256:512], -128.0, 127.0,
                                       ALU.max, ALU.min))
        v_seq(lambda e: e.tensor_single_scalar(dsc88[0:88, 768:1024],
                                              dsc88f[0:88, 512:768],
                                              dsc88f[0:88, 3:4], ALU.mult))
        nv = v_seq(lambda e, w_=w: e.tensor_single_scalar(
            dsc88[0:88, 0:256], dsc88[0:88, 768:1024],
            psc[0:88, 4 + w_:5 + w_], ALU.add))
        W(P.g, ve, nv)
        dma(P.g, q_g, ds_h[w].rearrange("(r c) -> r c", c=256),
            dsc88[0:88, 0:256])
        prev_ds_store = fence(P.g, q_g)
        if dbg:
            dma(P.g, q_g, dbg_out["d88f"][w], dsc88f[0:88, :])
            prev_ds_store = dma(P.g, q_g, dbg_out["d88b"][w],
                                dsc88[0:88, :])
    g_dsdone = q_g.n

    barrier()
    if ph <= 0:
        finish_early()
        return nc

    W(P.y, q_g, g_w)
    n_xb = dma(P.y, q_in, U[:, 0:32768].rearrange("p (i c) -> p i c", c=2048),
               xbf_h[:].rearrange("(i p) c -> p i c", p=128))
    n_xb = fence(P.y, q_in)

    passB = {"wp_ready": [None] * len(DT), "tr_done": [None] * len(DT),
             "stg_stored": [None] * len(DT)}
    tr_ev_ring = []
    last_di_of_w = {w: max(di for di in range(len(DT)) if DT[di][0] == w)
                    for w in range(3)}
    cc_agd = {}
    cc_tail = [None]

    def emit_passB_tile(di):
        w, wd, r0, p, c0, ncols, segw, s1c0 = DT[di]
        nseg = ncols // 64
        buf = di % 2
        if di >= 2:
            W(P.y, ve, passB["wp_ready"][di - 2])
        n_ld = dma(P.y, q_in, dwt[0:p, 2048 * buf:2048 * buf + ncols],
                   wd[r0:r0 + p, c0:c0 + ncols])
        n_ds = dma(P.y, q_in, dst2[0:p, 32 * buf:32 * buf + nseg],
                   ds_h[w].rearrange("(r c) -> r c", c=segw)
                   [r0:r0 + p, c0 // 64:c0 // 64 + nseg])
        fence(P.y, q_in)
        n_ld = n_ds + 16
        s1t = ds1[0:p, s1c0:s1c0 + nseg]
        v_seq(lambda e, s=s1t, p_=p, ns=nseg:
              e.tensor_copy(ds1f[0:p_, 0:ns], s))
        v_seq(lambda e, p_=p, ns=nseg: e.reciprocal(dsrec[0:p_, 0:ns],
                                                    ds1f[0:p_, 0:ns]))
        W(P.v, q_in, n_ld)
        W(P.v, ve, ve.n)
        for g in range(nseg):
            nlast = v_op(lambda e, p_=p, b=buf, g_=g: e.tensor_single_scalar(
                dplane[0:p_, 64 * g_:64 * g_ + 64],
                dwt[0:p_, 2048 * b + 64 * g_:2048 * b + 64 * g_ + 64],
                dsrec[0:p_, g_:g_ + 1], ALU.mult), inc=(g == nseg - 1))
        W(P.v, ve, nlast)
        v_op(lambda e, p_=p, nc_=ncols: e.memset(dacc[0:p_, 0:nc_], -1.0))
        for j in range(15):
            v_op(lambda e, p_=p, nc_=ncols, b=buf, tj=NF4_T[j], dj=NF4_D[j]:
                 e.tensor_scalar(dwt[0:p_, 2048 * b:2048 * b + nc_],
                                 dplane[0:p_, 0:nc_], tj, dj,
                                 ALU.is_ge, ALU.mult))
            v_op(lambda e, p_=p, nc_=ncols, b=buf: e.tensor_tensor(
                dacc[0:p_, 0:nc_], dacc[0:p_, 0:nc_],
                dwt[0:p_, 2048 * b:2048 * b + nc_], ALU.add))
        if di >= 1:
            W(P.v, pe, passB["tr_done"][di - 1])
        W(P.v, q_in, n_ds + 16)
        v_seq(lambda e, p_=p, ns=nseg, b=buf: e.tensor_copy(
            dstf[0:p_, 0:ns], dst2[0:p_, 32 * b:32 * b + ns]))
        W(P.v, ve, ve.n)
        nwp = None
        for g in range(nseg):
            nwp = v_op(lambda e, p_=p, g_=g: e.tensor_single_scalar(
                dwp[0:p_, 64 * g_:64 * g_ + 64],
                dacc[0:p_, 64 * g_:64 * g_ + 64],
                dstf[0:p_, g_:g_ + 1], ALU.mult), inc=(g == nseg - 1))
        passB["wp_ready"][di] = nwp

    def emit_transpose_batch(di):
        w, wd, r0, p, c0, ncols, segw, s1c0 = DT[di]
        nch = ncols // 128
        W(P.t, ve, passB["wp_ready"][di])
        prev_store = passB["stg_stored"][di - 1] if di >= 1 else None
        for ci in range(nch):
            bank = 5 + ci % 3
            if len(tr_ev_ring) >= 3:
                W(P.t, ve, tr_ev_ring[-3])
            n_tr = pe.inc(1)
            P.t(lambda e, p_=p, ci_=ci, bk=bank:
                e.transpose(PS[bk][:, 0:p_],
                            dwp[0:p_, 128 * ci_:128 * (ci_ + 1)],
                            eyef[0:p_, 0:p_]).then_inc(pe.h, 1))
            W(P.v, pe, n_tr)
            if ci == 0 and prev_store:
                W(P.v, q_g, prev_store)
            nev = v_op(lambda e, p_=p, ci_=ci, bk=bank: e.tensor_copy(
                dstg[:, p_ * ci_:p_ * (ci_ + 1)],
                PS[bk][:, 0:p_]), inc=True)
            tr_ev_ring.append(nev)
        passB["tr_done"][di] = pe.n
        W(P.g, ve, tr_ev_ring[-1])
        if w < 2:
            dest = agd_in[w].rearrange("(i p f) -> p i f", i=16, p=128)[
                :, :, r0:r0 + p]
            src = dstg[:, 0:16 * p].rearrange("p (i f) -> p i f", f=p)
        else:
            j0 = c0 // 128
            dest = agd_in[2].rearrange("(j p o) -> p j o", j=44, p=128)[
                :, j0:j0 + nch, r0:r0 + p]
            src = dstg[:, 0:nch * p].rearrange("p (j o) -> p j o", o=p)
        passB["stg_stored"][di] = dma(P.g, q_g, dest, src)

    W(P.t, q_in, n_xb)
    W(P.t, q_g, g_small)

    n_acts = None
    for tq in range(4):
        for i in range(16):
            t_mm(PS[4][0:80, :], ballt_sb[:, 80 * i:80 * (i + 1)],
                 xv(i)[:, 512 * tq:512 * (tq + 1)], i == 0, i == 15,
                 inc=(i == 15))
        n_pe = pe.n
        W(P.v, pe, n_pe)
        n_acts = v_op(lambda e, tq_=tq: e.tensor_single_scalar(
            acts[0:80, 512 * tq_:512 * (tq_ + 1)], PS[4][0:80, :], 0.5,
            ALU.mult), inc=True)
        if tq < 3:
            W(P.t, ve, n_acts)

    def load_wcol(put, wsel, j, buf):
        dstap = (wcolA if buf == 0 else wcolB)[:, 0:2048].rearrange(
            "p (i c) -> p i c", c=128)
        srcap = agw_out[wsel].rearrange("(i p) c -> p i c", p=128)[
            :, :, 128 * j:128 * (j + 1)]
        return dma(put, q_in, dstap, srcap)

    W(P.y, cc, agw_n[0])
    unit_no = [0]
    trq = list(range(len(DT)))
    emitted_B = [0]

    def maybe_interleave():
        u = unit_no[0]
        unit_no[0] += 1
        if p1sub < 5:
            return
        want_B = min(len(DT), 2 + u // 8)
        while emitted_B[0] < want_B:
            di = emitted_B[0]
            if di >= 1 and trq and trq[0] == di - 1:
                emit_transpose_batch(trq.pop(0))
            emit_passB_tile(di)
            emitted_B[0] += 1

    psum_user = [None] * 8
    stg_store = {}
    wcol_pe = {}

    def qk_units(wsel, kind, tqs, dram_dst, bcol0, prev_kind):
        for j in range(16):
            buf = j % 2
            if j == 0:
                W(P.y, cc, agw_n[wsel] + 1)
            if j >= 2:
                W(P.y, pe, wcol_pe[(kind, j - 2)])
            elif prev_kind is not None:
                W(P.y, pe, wcol_pe[(prev_kind, 14 + j)])
            n_w = load_wcol(P.y, wsel, j, buf)
            if j == 15:
                fence(P.y, q_in)
            for tq in tqs:
                u = unit_no[0]
                ps = u % 4
                if psum_user[ps] is not None:
                    W(P.t, psum_user[ps][0], psum_user[ps][1])
                W(P.t, q_in, n_w + 16)
                wc = wcolA if buf == 0 else wcolB
                for i in range(16):
                    t_mm(PS[ps][:, :], wc[:, 128 * i:128 * (i + 1)],
                         xv(i)[:, 512 * tq:512 * (tq + 1)], i == 0, False)
                arow = 0 if kind == "q" else 32
                npe = t_mm(PS[ps][:, :],
                           at4_sb[arow:arow + 16, 128 * j:128 * j + 128],
                           acts[arow:arow + 16, 512 * tq:512 * (tq + 1)],
                           False, True, inc=True)
                wcol_pe[(kind, j)] = npe
                W(P.s, pe, npe)
                W(P.s, q_sc, stg_store.get(u - 3))
                stg = pt3[:, 512 * (u % 3):512 * (u % 3) + 512]
                nse = s_op(lambda e, ps_=ps, st=stg, bj=bcol0 + j:
                           e.activation(st, PS[ps_][:, :], AF.Identity,
                                        bias=bcols_sb[:, bj:bj + 1]),
                           inc=True)
                psum_user[ps] = (se, nse)
                W(P.s, se, nse)
                stg_store[u] = dma(P.s, q_sc,
                                   dram_dst[128 * j:128 * (j + 1),
                                            512 * tq:512 * (tq + 1)], stg)
                maybe_interleave()

    if p1sub >= 2:
        qk_units(0, "q", (0, 1), q_hbm, 0, None)
    if p1sub >= 3:
        qk_units(1, "k", (0, 1, 2, 3), k_hbm, 16, "q")
    if p1sub < 4:
        barrier()
        finish_early()
        return nc

    vt_pe_last = [wcol_pe[("k", 15)]]

    for oq in range(4):
        if oq == 0:
            W(P.y, cc, agw_n[2] + 1)
        W(P.y, pe, vt_pe_last[0])
        dma(P.y, q_in,
            wcolA[:].rearrange("p (i c) -> p i c", c=512),
            agw_out[2][0:1024, :].rearrange("(i p) c -> p i c", p=128)[
                :, :, 512 * oq:512 * (oq + 1)])
        nldB = dma(P.y, q_in,
                   wcolB[:].rearrange("p (i c) -> p i c", c=512),
                   agw_out[2][1024:2048, :].rearrange("(i p) c -> p i c",
                                                      p=128)[
                       :, :, 512 * oq:512 * (oq + 1)])
        fence(P.y, q_in)
        for tc in range(16):
            u = unit_no[0]
            ps = u % 4
            if psum_user[ps] is not None:
                W(P.t, psum_user[ps][0], psum_user[ps][1])
            W(P.t, q_in, nldB + 16)
            for i in range(16):
                wc = wcolA if i < 8 else wcolB
                ii = i % 8
                t_mm(PS[ps][:, :], xv(i)[:, 128 * tc:128 * (tc + 1)],
                     wc[:, 512 * ii:512 * (ii + 1)], i == 0, False)
            npe = t_mm(PS[ps][:, :], acts[64:80, 128 * tc:128 * (tc + 1)],
                       at4_sb[64:80, 512 * oq:512 * oq + 512],
                       False, True, inc=True)
            vt_pe_last[0] = npe
            W(P.v, pe, npe)
            W(P.v, q_in, n_bvrep)
            W(P.v, q_sc, stg_store.get(u - 3))
            stg = pt3[:, 512 * (u % 3):512 * (u % 3) + 512]
            nve = v_op(lambda e, ps_=ps, st=stg, o_=oq: e.tensor_tensor(
                st, PS[ps_][:, :],
                bvrep_sb[:, 512 * o_:512 * (o_ + 1)], ALU.add), inc=True)
            psum_user[ps] = (ve, nve)
            W(P.s, ve, nve)
            stg_store[u] = dma(P.s, q_sc,
                               vt_hbm[128 * tc:128 * (tc + 1),
                                      512 * oq:512 * (oq + 1)], stg)
            maybe_interleave()

    if p1sub < 5:
        barrier()
        finish_early()
        return nc
    while emitted_B[0] < len(DT):
        di = emitted_B[0]
        if di >= 1 and trq and trq[0] == di - 1:
            emit_transpose_batch(trq.pop(0))
        emit_passB_tile(di)
        emitted_B[0] += 1
    while trq:
        emit_transpose_batch(trq.pop(0))

    for w in range(3):
        W(P.g, q_g, passB["stg_stored"][last_di_of_w[w]])
    W(P.g, q_g, fence(P.g, q_g))
    for w in range(3):
        cc_agd[w] = collective(agd_in[w].opt(), agd_out[w].opt())
    P.g(lambda e, v=cc_agd[2]: e.wait_ge(cc.h, v))

    f2_in, f2_g, f2_sc = fence(P.y, q_in), fence(P.g, q_g), fence(P.s, q_sc)
    barrier()
    if ph <= 1:
        finish_early()
        return nc

    W(P.y, q_sc, f2_sc)
    unit_pe = {}
    exp_se = {}
    av_done_v = [None]
    sm_done_s = [None]
    att_store = {}
    sum_store = {}
    pt_user = [None, None, None]
    kv_pe = {}
    score_pe = {}

    for u in range(32):
        h, f = u // 2, u % 2
        kbuf = h % 2
        if f == 0:
            if h >= 2:
                W(P.y, pe, kv_pe.get(h - 2))
            dma(P.y, q_in, kh2[:, 2048 * kbuf:2048 * (kbuf + 1)],
                k_hbm[128 * h:128 * (h + 1), :])
            dma(P.y, q_in, vt2[:, 2048 * kbuf:2048 * (kbuf + 1)]
                .rearrange("p (t c) -> p t c", c=128),
                vt_hbm[:, 128 * h:128 * (h + 1)]
                .rearrange("(t p) c -> p t c", p=128))
        qbuf = u % 2
        W(P.y, pe, unit_pe.get(u - 2))
        n_q = dma(P.y, q_in, qh2[:, 512 * qbuf:512 * (qbuf + 1)],
                  q_hbm[128 * h:128 * (h + 1), 512 * f:512 * (f + 1)])
        W(P.t, q_in, n_q + 32)

        def score(kt):
            gkt = 16 * u + kt
            score_pe[gkt] = t_mm(
                PS[kt % 2][:, :],
                kh2[:, 2048 * kbuf + 128 * kt:2048 * kbuf + 128 * kt + 128],
                qh2[:, 512 * qbuf:512 * (qbuf + 1)], True, True, inc=True)

        for kt in (0, 1):
            W(P.t, se, exp_se.get(16 * u + kt - 2))
            score(kt)
        W(P.t, ve, av_done_v[0])
        W(P.t, se, sm_done_s[0])
        W(P.v, q_g, att_store.get(u - 2))
        W(P.s, q_sc, sum_store.get(u - 2))
        for kt in range(16):
            gkt = 16 * u + kt
            slot = gkt % 3
            W(P.s, pe, score_pe[gkt])
            W(P.s, pe, pt_user[slot])
            nse = s_op(lambda e, sps_=kt % 2, sl=slot: e.activation(
                pt3[:, 512 * sl:512 * (sl + 1)], PS[sps_][:, :], AF.Exp,
                bias=psc[:, 15:16], scale=SCALE), inc=True)
            exp_se[gkt] = nse
            W(P.t, se, nse)
            t_mm(PS[2][:, :],
                 vt2[:, 2048 * kbuf + 128 * kt:2048 * kbuf + 128 * kt + 128],
                 pt3[:, 512 * slot:512 * (slot + 1)], kt == 0, kt == 15)
            npe2 = t_mm(PS[3][0:1, :], onesb[:, :],
                        pt3[:, 512 * slot:512 * (slot + 1)], kt == 0,
                        kt == 15, inc=True)
            pt_user[slot] = npe2
            if kt + 2 <= 15:
                score(kt + 2)
        unit_pe[u] = pe.n
        kv_pe[h] = pe.n
        W(P.v, pe, pe.n)
        nev = v_op(lambda e, u_=u: e.tensor_copy(
            sA[:, 512 * (u_ % 2):512 * (u_ % 2) + 512], PS[2][:, :]),
            inc=True)
        av_done_v[0] = nev
        W(P.g, ve, nev)
        att_store[u] = dma(P.g, q_g,
                           att_h[128 * h:128 * (h + 1), 512 * f:512 * (f + 1)],
                           sA[:, 512 * (u % 2):512 * (u % 2) + 512])
        W(P.s, pe, pe.n)
        nss = s_op(lambda e, u_=u: e.copy(
            lnrow[0:1, 512 * (u_ % 2):512 * (u_ % 2) + 512],
            PS[3][0:1, :]), inc=True)
        sm_done_s[0] = nss
        W(P.s, se, nss)
        sum_store[u] = dma(P.s, q_sc,
                           sums_h[h, 512 * f:512 * (f + 1)].unsqueeze(0),
                           lnrow[0:1, 512 * (u % 2):512 * (u % 2) + 512])

    fence(P.y, q_in)
    f3_in, f3_g, f3_sc = fence(P.y, q_in), fence(P.g, q_g), fence(P.s, q_sc)
    barrier()
    if ph <= 2:
        finish_early()
        return nc

    W(P.y, q_sc, f3_sc)
    W(P.y, q_g, f3_g)
    n_sums = dma(P.y, q_in, sums16, sums_h[:])
    n_sums = fence(P.y, q_in)
    W(P.v, q_in, n_sums)
    nv_rec = v_op(lambda e: e.reciprocal(recip16, sums16), inc=True)
    W(P.g, ve, nv_rec)
    g_rcp = dma(P.g, q_g, rcp_h[:], recip16)
    g_rcp = fence(P.g, q_g)

    attb = U
    n_attb_rep_pe = [0]
    norm_ev = {}
    W(P.t, ve, nv_rec)
    for u in range(32):
        h, f = u // 2, u % 2
        W(P.y, ve, norm_ev.get(u - 2))
        nld = dma(P.y, q_in, sC[:, 512 * (u % 2):512 * (u % 2) + 512],
                  att_h[128 * h:128 * (h + 1), 512 * f:512 * (f + 1)])
        W(P.y, q_g, g_rcp)
        nldr = dma(P.y, q_in, lnrow[0:1, 512 * (u % 2):512 * (u % 2) + 512],
                   rcp_h[h, 512 * f:512 * (f + 1)].unsqueeze(0))
        if u == 31:
            nldr = fence(P.y, q_in)
        nld = nldr
        if u >= 2:
            W(P.t, ve, norm_ev.get(u - 2))
        W(P.t, q_in, nldr)
        npe = t_mm(PS[2 + u % 2][:, :], onesrow[:],
                   lnrow[0:1, 512 * (u % 2):512 * (u % 2) + 512], True, True,
                   inc=True)
        W(P.v, pe, npe)
        W(P.v, q_in, nld)
        nev = v_op(lambda e, u_=u, h_=h, f_=f: e.tensor_tensor(
            attb[:, 1024 * h_ + 512 * f_:1024 * h_ + 512 * f_ + 512],
            sC[:, 512 * (u_ % 2):512 * (u_ % 2) + 512],
            PS[2 + u_ % 2][:, :], ALU.mult), inc=True)
        norm_ev[u] = nev
    n_attb = norm_ev[31]

    W(P.t, ve, n_attb)
    n_actso = None
    for tq in range(2):
        for i in range(16):
            t_mm(PS[4][0:16, :], bot_sb[:, 16 * i:16 * (i + 1)],
                 attb[:, 1024 * i + 512 * tq:1024 * i + 512 * tq + 512],
                 i == 0, i == 15, inc=(i == 15))
        W(P.v, pe, pe.n)
        n_actso = v_op(lambda e, tq_=tq: e.tensor_single_scalar(
            acts_o[0:16, 512 * tq_:512 * (tq_ + 1)], PS[4][0:16, :], 0.5,
            ALU.mult), inc=True)
        if tq == 0:
            W(P.t, ve, n_actso)

    o_pe = {}
    o_ev = {}
    o_sq = {}
    o_st = {}
    o_stats_pe = {}
    W(P.t, ve, n_actso)
    for j in range(16):
        buf = j % 2
        if j == 0:
            W(P.y, cc, agw_n[3] + 1)
        if j >= 2:
            W(P.y, pe, o_stats_pe.get(2 * (j - 2) + 1, None))
        n_w = load_wcol(P.y, 3, j, buf)
        W(P.y, ve, o_ev.get(2 * (j - 2) + 1))
        if j < 2:
            W(P.y, ve, nv_rec)
            W(P.y, q_g, g_rcp)
            W(P.y, pe, n_attb_rep_pe[0])
        nldx = dma(P.y, q_in, rx2[:, 1024 * buf:1024 * buf + 1024],
                   xt[128 * j:128 * (j + 1), 0:1024])
        if j == 15:
            fence(P.y, q_in)
        for f in range(2):
            u = 2 * j + f
            ps = u % 2
            W(P.t, ve, o_ev.get(u - 2))
            W(P.t, q_in, n_w + 16)
            wc = wcolA if buf == 0 else wcolB
            for i in range(16):
                t_mm(PS[ps][:, :], wc[:, 128 * i:128 * (i + 1)],
                     attb[:, 1024 * i + 512 * f:1024 * i + 512 * f + 512],
                     i == 0, False)
            npe = t_mm(PS[ps][:, :],
                       at4_sb[0:16, 2048 + 128 * j:2048 + 128 * j + 128],
                       acts_o[0:16, 512 * f:512 * (f + 1)], False, True,
                       inc=True)
            o_pe[u] = npe
            W(P.v, pe, npe)
            W(P.v, q_in, nldx + 16)
            W(P.v, q_g, o_st.get(u - 2))
            W(P.v, se, o_sq.get(u - 2))
            W(P.v, pe, o_stats_pe.get(u - 2))
            nev = v_op(lambda e, ps_=ps, j_=j, f_=f, b=buf:
                       e.scalar_tensor_tensor(
                           sA[:, 512 * ps_:512 * ps_ + 512],
                           PS[ps_][:, :], bcols_sb[:, 32 + j_:33 + j_],
                           rx2[:, 1024 * b + 512 * f_:
                               1024 * b + 512 * f_ + 512],
                           ALU.add, ALU.add), inc=True)
            o_ev[u] = nev
            W(P.g, ve, nev)
            o_st[u] = dma(P.g, q_g,
                          r_h[128 * j:128 * (j + 1), 512 * f:512 * (f + 1)],
                          sA[:, 512 * ps:512 * ps + 512])
            W(P.s, ve, nev)
            nsq = s_op(lambda e, ps_=ps: e.activation(
                sB[:, 512 * ps_:512 * ps_ + 512],
                sA[:, 512 * ps_:512 * ps_ + 512], AF.Square), inc=True)
            o_sq[u] = nsq
            W(P.t, ve, nev)
            W(P.t, se, nsq)
            t_mm(PS[4 + f][0:1, :], onesf[:], sA[:, 512 * ps:512 * ps + 512],
                 j == 0, j == 15)
            npe3 = t_mm(PS[6 + f][0:1, :], onesf[:],
                        sB[:, 512 * ps:512 * ps + 512], j == 0, j == 15,
                        inc=True)
            o_stats_pe[u] = npe3

    def ln_tail():
        W(P.v, pe, pe.n)
        for i, ps in enumerate((4, 5, 6, 7)):
            v_op(lambda e, i_=i, ps_=ps: e.tensor_copy(
                lnrow[0:1, 512 * i_:512 * (i_ + 1)], PS[ps_][0:1, :]))
        v_seq(lambda e: e.tensor_single_scalar(
            lnrow[0:1, 0:1024], lnrow[0:1, 0:1024], 1.0 / D, ALU.mult))
        v_seq(lambda e: e.tensor_single_scalar(
            lnrow[0:1, 1024:2048], lnrow[0:1, 1024:2048], 1.0 / D, ALU.mult))
        v_seq(lambda e: e.tensor_tensor(
            sC[0:1, 0:1024], lnrow[0:1, 0:1024], lnrow[0:1, 0:1024],
            ALU.mult))
        v_seq(lambda e: e.tensor_tensor(
            lnrow[0:1, 1024:2048], lnrow[0:1, 1024:2048], sC[0:1, 0:1024],
            ALU.subtract))
        nv1 = v_seq(lambda e: e.tensor_single_scalar(
            lnrow[0:1, 1024:2048], lnrow[0:1, 1024:2048], EPS, ALU.add))
        W(P.s, ve, nv1)
        ns1 = s_op(lambda e: e.activation(sC[0:1, 0:1024],
                                          lnrow[0:1, 1024:2048], AF.Sqrt),
                   inc=True)
        W(P.v, se, ns1)
        nv2 = v_op(lambda e: e.reciprocal(lnrow[0:1, 1024:2048],
                                          sC[0:1, 0:1024]), inc=True)
        W(P.t, ve, nv2)
        for f in range(2):
            t_mm(PS[4 + f][:, :], onesrow[:],
                 lnrow[0:1, 512 * f:512 * (f + 1)], True, True)
        n = None
        for f in range(2):
            n = t_mm(PS[6 + f][:, :], onesrow[:],
                     lnrow[0:1, 1024 + 512 * f:1024 + 512 * (f + 1)],
                     True, True, inc=(f == 1))
        return n

    n_rep1 = ln_tail()

    f_rst = fence(P.g, q_g)
    nrm_ev = {}
    nrm_st = {}
    for i in range(16):
        buf = i % 2
        W(P.y, ve, nrm_ev.get(i - 2))
        if i < 2:
            W(P.y, ve, o_ev.get(2 * (14 + i) + 1))
        W(P.y, q_g, f_rst)
        nld = dma(P.y, q_in, rx2[:, 1024 * buf:1024 * buf + 1024],
                  r_h[128 * i:128 * (i + 1), :])
        if i == 15:
            nld = fence(P.y, q_in)
        for f in range(2):
            W(P.v, pe, n_rep1)
            W(P.v, q_in, nld + 16 if i < 15 else nld)
            W(P.v, q_g, nrm_st.get((i - 1, f)))
            v_op(lambda e, b=buf, f_=f: e.tensor_tensor(
                sB[:, 512 * f_:512 * f_ + 512],
                rx2[:, 1024 * b + 512 * f_:1024 * b + 512 * f_ + 512],
                PS[4 + f_][:, :], ALU.subtract))
            v_op(lambda e, f_=f: e.tensor_tensor(
                sA[:, 512 * f_:512 * f_ + 512], sB[:, 512 * f_:512 * f_ + 512],
                PS[6 + f_][:, :], ALU.mult))
            nev = v_op(lambda e, i_=i, f_=f: e.tensor_copy(
                x2b[:, 1024 * i_ + 512 * f_:1024 * i_ + 512 * f_ + 512],
                sA[:, 512 * f_:512 * f_ + 512]), inc=True)
            nrm_ev[i] = nev
            W(P.g, ve, nev)
            nrm_st[(i, f)] = dma(P.g, q_g,
                                 x2f_h[128 * i:128 * (i + 1),
                                       512 * f:512 * (f + 1)],
                                 sA[:, 512 * f:512 * f + 512])

    f4_in, f4_g, f4_sc = fence(P.y, q_in), fence(P.g, q_g), fence(P.s, q_sc)
    barrier()
    if ph <= 3:
        finish_early()
        return nc

    W(P.t, cc, cc_agd[2])
    W(P.y, cc, cc_agd[2])

    def load_fcol(put, w, j, dstbuf):
        f0, f1 = 128 * j, 128 * (j + 1)
        r0, r1 = f0 // 704, (f1 - 1) // 704
        n = None
        for r in range(r0, r1 + 1):
            lo = max(f0, 704 * r)
            hi = min(f1, 704 * (r + 1))
            src = agd_out[w][r].rearrange("(i p f) -> p i f", i=16, p=128)[
                :, :, lo - 704 * r:hi - 704 * r]
            dst = dstbuf.rearrange("p (i c) -> p i c", c=128)[
                :, :, lo - f0:hi - f0]
            n = dma(put, q_in, dst, src)
        return n

    hv = U
    ab_user = [None] * 8
    ab_pe = {}
    ab_h = {}
    for j in range(44):
        buf = j % 2
        if j >= 2:
            W(P.y, pe, ab_pe.get(j - 2))
        load_fcol(P.y, 0, j, (wcolA if buf == 0 else wcolB)[:, 0:2048])
        w2c = rx2[:, 0:1024].bitcast(BF) if buf == 0 \
            else rx2[:, 1024:2048].bitcast(BF)
        n_w2 = load_fcol(P.y, 1, j, w2c)
        if j == 43:
            n_w2 = fence(P.y, q_in)
        for f in range(2):
            pa = 4 * (j % 2) + f
            pb = 4 * (j % 2) + 2 + f
            W(P.t, ve, ab_user[pa])
            W(P.t, ve, ab_user[pb])
            W(P.t, q_in, n_w2 + 16 if j < 43 else n_w2)
            wc1 = (wcolA if buf == 0 else wcolB)[:]
            wc2 = rx2[:, 0:1024].bitcast(BF) if buf == 0 \
                else rx2[:, 1024:2048].bitcast(BF)
            for i in range(16):
                t_mm(PS[pa][:, :], wc1[:, 128 * i:128 * (i + 1)],
                     x2b[:, 1024 * i + 512 * f:1024 * i + 512 * f + 512],
                     i == 0, i == 15)
            for i in range(16):
                t_mm(PS[pb][:, :], wc2[:, 128 * i:128 * (i + 1)],
                     x2b[:, 1024 * i + 512 * f:1024 * i + 512 * f + 512],
                     i == 0, i == 15, inc=(i == 15))
            npe = pe.n
            ab_pe[j] = npe
            W(P.s, pe, npe)
            W(P.s, ve, ab_h.get((j - 1, f)))
            nsi = s_op(lambda e, pa_=pa, f_=f: e.activation(
                sA[:, 512 * f_:512 * f_ + 512], PS[pa_][:, :], AF.Silu),
                inc=True)
            W(P.v, se, nsi)
            nev = v_op(lambda e, pb_=pb, j_=j, f_=f: e.tensor_tensor(
                hv[:, 1024 * j_ + 512 * f_:1024 * j_ + 512 * f_ + 512],
                sA[:, 512 * f_:512 * f_ + 512], PS[pb_][:, :], ALU.mult),
                inc=True)
            ab_user[pa] = nev
            ab_user[pb] = nev
            ab_h[(j, f)] = nev

    f5_in, f5_g, f5_sc = fence(P.y, q_in), fence(P.g, q_g), fence(P.s, q_sc)
    barrier()
    if ph <= 4:
        finish_early()
        return nc

    W(P.t, cc, cc_agd[2])
    W(P.y, cc, cc_agd[2])
    W(P.y, q_g, f4_g)
    w3_pe = {}
    w3_ev = {}
    w3_sq = {}
    w3_st = {}
    w3_stats_pe = {}
    for k in range(16):
        buf = k % 2
        if k >= 2:
            W(P.y, pe, w3_stats_pe.get(2 * (k - 2) + 1))
        dst = x2b[:, 5632 * buf:5632 * (buf + 1)].rearrange(
            "p (j c) -> p j c", c=128)
        src = agd_out[2][k // 2].rearrange("(j p o) -> p j o", j=44, p=128)[
            :, :, 128 * (k % 2):128 * (k % 2) + 128]
        n_w3 = dma(P.y, q_in, dst, src)
        W(P.y, ve, w3_ev.get(2 * (k - 2) + 1))
        nldx = dma(P.y, q_in, rx2[:, 1024 * buf:1024 * buf + 1024],
                   x2f_h[128 * k:128 * (k + 1), :])
        if k == 15:
            fence(P.y, q_in)
        for f in range(2):
            u = 2 * k + f
            ps = 2 * (k % 2) + f
            W(P.t, ve, w3_ev.get(u - 4))
            W(P.t, q_in, n_w3 + 16)
            wc = x2b[:, 5632 * buf:5632 * (buf + 1)]
            for j in range(44):
                t_mm(PS[ps][:, :], wc[:, 128 * j:128 * (j + 1)],
                     hv[:, 1024 * j + 512 * f:1024 * j + 512 * f + 512],
                     j == 0, j == 43, inc=(j == 43))
            npe = pe.n
            w3_pe[u] = npe
            W(P.v, pe, npe)
            W(P.v, q_in, nldx + 16)
            W(P.v, q_g, w3_st.get(u - 2))
            W(P.v, se, w3_sq.get(u - 2))
            W(P.v, pe, w3_stats_pe.get(u - 2))
            nev = v_op(lambda e, ps_=ps, b=buf, f_=f: e.tensor_tensor(
                sA[:, 512 * f_:512 * f_ + 512], PS[ps_][:, :],
                rx2[:, 1024 * b + 512 * f_:1024 * b + 512 * f_ + 512],
                ALU.add), inc=True)
            w3_ev[u] = nev
            W(P.g, ve, nev)
            w3_st[u] = dma(P.g, q_g,
                           r2_h[128 * k:128 * (k + 1), 512 * f:512 * (f + 1)],
                           sA[:, 512 * f:512 * f + 512])
            W(P.s, ve, nev)
            nsq = s_op(lambda e, f_=f: e.activation(
                sB[:, 512 * f_:512 * f_ + 512],
                sA[:, 512 * f_:512 * f_ + 512], AF.Square), inc=True)
            w3_sq[u] = nsq
            W(P.t, ve, nev)
            W(P.t, se, nsq)
            t_mm(PS[4 + f][0:1, :], onesf[:], sA[:, 512 * f:512 * f + 512],
                 k == 0, k == 15)
            npe3 = t_mm(PS[6 + f][0:1, :], onesf[:],
                        sB[:, 512 * f:512 * f + 512], k == 0, k == 15,
                        inc=True)
            w3_stats_pe[u] = npe3

    n_rep2 = ln_tail()

    f_r2 = fence(P.g, q_g)
    out_ev = {}
    out_st = {}
    for k in range(16):
        buf = k % 2
        W(P.y, ve, out_ev.get(k - 2))
        if k < 2:
            W(P.y, ve, w3_ev.get(2 * (14 + k) + 1))
        W(P.y, q_g, f_r2)
        nld = dma(P.y, q_in, rx2[:, 1024 * buf:1024 * buf + 1024],
                  r2_h[128 * k:128 * (k + 1), :])
        fence(P.y, q_in)
        for f in range(2):
            W(P.v, pe, n_rep2)
            W(P.v, q_in, nld + 16)
            W(P.v, q_g, out_st.get((k - 1, f)))
            v_op(lambda e, b=buf, f_=f: e.tensor_tensor(
                sB[:, 512 * f_:512 * f_ + 512],
                rx2[:, 1024 * b + 512 * f_:1024 * b + 512 * f_ + 512],
                PS[4 + f_][:, :], ALU.subtract))
            nev = v_op(lambda e, f_=f: e.tensor_tensor(
                sA[:, 512 * f_:512 * f_ + 512],
                sB[:, 512 * f_:512 * f_ + 512],
                PS[6 + f_][:, :], ALU.mult), inc=True)
            out_ev[k] = nev
            W(P.g, ve, nev)
            out_st[(k, f)] = dma(P.g, q_g,
                                 out[128 * k:128 * (k + 1),
                                     512 * f:512 * (f + 1)],
                                 sA[:, 512 * f:512 * f + 512])
    dma(P.y, q_out, r_h[0:1, 0:1], dum[0:1, 0:1])
    P.y(lambda e: e.wait_ge(q_out.h, 16))
    P.y(lambda e, v=out_st[(15, 1)]: e.wait_ge(q_g.h, v))
    if dbg:
        srcs = {"q_hbm": q_hbm, "k_hbm": k_hbm, "vt_hbm": vt_hbm,
                "att_h": att_h, "sums_h": sums_h, "r_h": r_h,
                "x2f_h": x2f_h, "r2_h": r2_h, "ds_h": ds_h,
                "agd0_in": agd_in[0], "agd2_in": agd_in[2], "s1_h": s1_h}
        W(P.g, q_out, 16 * 32)
        for nm, ap_ in srcs.items():
            dma(P.g, q_g, dbg_out[nm], ap_)
        dma(P.g, q_g, dbg_out["acts_d"], acts[0:80, :])
        nacts = dma(P.g, q_g, dbg_out["psc_d"], psc[:])
        P.g(lambda e, v=nacts: e.wait_ge(q_g.h, v))

    emit_block()
    return nc


def shard_inputs(inputs):
    f32 = np.float32
    x = np.ascontiguousarray(np.asarray(inputs["L0_attn_x"], dtype=f32))
    Wq, Wk = inputs["L0_attn_Wq"], inputs["L0_attn_Wk"]
    Wv, Wo = inputs["L0_attn_Wv"], inputs["L0_attn_Wo"]
    bq, bk = inputs["L0_attn_bq"], inputs["L0_attn_bk"]
    bv, bo = inputs["L0_attn_bv"], inputs["L0_attn_bo"]
    Bq, Bk = inputs["L0_attn_Bq"], inputs["L0_attn_Bk"]
    Bv, Bo = inputs["L0_attn_Bv"], inputs["L0_attn_Bo"]
    Aq, Ak = inputs["L0_attn_Aq"], inputs["L0_attn_Ak"]
    Av, Ao = inputs["L0_attn_Av"], inputs["L0_attn_Ao"]
    w1 = np.ascontiguousarray(np.asarray(inputs["L0_ffn_weight1"], dtype=BF16))
    w2 = np.ascontiguousarray(np.asarray(inputs["L0_ffn_weight2"], dtype=BF16))
    w3 = np.ascontiguousarray(np.asarray(inputs["L0_ffn_weight3"], dtype=BF16))

    WT = np.stack([np.ascontiguousarray(np.asarray(W, f32).T)
                   for W in (Wq, Wk, Wv, Wo)])
    bcols = np.zeros((128, 48), f32)
    for wi, b in enumerate((bq, bk, bo)):
        bcols[:, 16 * wi:16 * (wi + 1)] = np.asarray(b, f32).reshape(16, 128).T
    ballt = np.zeros((D, 80), f32)
    for gi, B in zip((0, 32, 64), (Bq, Bk, Bv)):
        ballt[:, gi:gi + 16] = np.asarray(B, f32).T
    ballt = np.ascontiguousarray(ballt)
    botm = np.ascontiguousarray(np.asarray(Bo, f32).T)
    at4m = np.zeros((80, 4096), f32)
    at4m[0:16, 0:2048] = np.asarray(Aq, f32).T
    at4m[0:16, 2048:4096] = np.asarray(Ao, f32).T
    at4m[32:48, 0:2048] = np.asarray(Ak, f32).T
    at4m[64:80, 0:2048] = np.asarray(Av, f32).T
    at4m = np.ascontiguousarray(at4m)
    eyefm = np.eye(128, dtype=f32)
    onesfm = np.ones((128, 1), f32)
    onesrowm = np.ones((1, 128), f32)
    bvrepm = np.ascontiguousarray(
        np.broadcast_to(np.asarray(bv, f32).reshape(1, D), (128, D)).copy())

    in_maps = []
    for c in range(8):
        b, hh = c // 2, c % 2
        xb = x[b]
        if hh == 1:
            xb = np.concatenate([xb[1024:], xb[:1024]], axis=0)
        m = {
            "xt": np.ascontiguousarray(xb.T),
            "wts": np.ascontiguousarray(WT[:, 256 * c:256 * (c + 1), :]),
            "bcols": bcols, "bvrep": bvrepm,
            "ballt": ballt, "bot": botm, "at4": at4m,
            "w1s": np.ascontiguousarray(w1[704 * c:704 * (c + 1)]),
            "w2s": np.ascontiguousarray(w2[704 * c:704 * (c + 1)]),
            "w3s": np.ascontiguousarray(w3[256 * c:256 * (c + 1)]),
            "eyef_d": eyefm, "onesf_d": onesfm, "onesrow_d": onesrowm,
        }
        in_maps.append(m)
    return in_maps


def kernel(**inputs):
    if "nc" not in _CACHED:
        _CACHED["nc"] = build_nc()
    nc = _CACHED["nc"]
    in_maps = shard_inputs(inputs)
    res = run_bass_kernel_spmd(nc, in_maps, core_ids=list(range(8)))
    full = np.zeros((4, TS, D), np.float32)
    for c in range(8):
        b, hh = c // 2, c % 2
        full[b, 1024 * hh:1024 * (hh + 1), :] = \
            np.asarray(res.results[c]["out"]).T
    return full
